# version 09
# baseline (speedup 1.0000x reference)
import contextlib
import numpy as np
import ml_dtypes
import concourse.bass as bass
import concourse.mybir as mybir
from concourse.bass_utils import run_bass_kernel_spmd

F32 = mybir.dt.float32
BF16 = mybir.dt.bfloat16
I32 = mybir.dt.int32
AF = mybir.ActivationFunctionType
ALU = mybir.AluOpType
AX = mybir.AxisListType
BF = ml_dtypes.bfloat16

NSEQ = 2
NT = 16
NCH = 8
D = 1024
T = 2048

ENGS = ("tensor", "scalar", "vector", "sync", "gpsimd")


class Sched:

    def __init__(self):
        self.ops = {e: [] for e in ENGS}
        self.cnt = {}

    def run(self, eng, fn, sem=None, amt=1):
        self.ops[eng].append(("inst", fn, sem, amt))
        if sem is not None:
            self.cnt[sem] = self.cnt.get(sem, 0) + amt
            return (sem, self.cnt[sem])
        return None

    def dma(self, eng, fn, sem):
        return self.run(eng, fn, sem=sem, amt=16)

    def wait(self, eng, tok):
        if tok is None:
            return
        sem, val = tok
        if val <= 0:
            return
        assert self.cnt.get(sem, 0) >= val, f"wait on future token {sem}={val}"
        ops = self.ops[eng]
        if ops and ops[-1][0] == "wait" and ops[-1][1] == sem:
            ops[-1] = ("wait", sem, max(ops[-1][2], val))
        else:
            ops.append(("wait", sem, val))

    def drain(self, eng):
        self.ops[eng].append(("drain",))

    def barrier(self, toks):
        for e in ENGS:
            for tok in toks:
                self.wait(e, tok)

    def now(self, sem):
        return (sem, self.cnt.get(sem, 0))

    def realize(self, block, sems):
        def make(oplist):
            def body(e):
                for op in oplist:
                    if op[0] == "wait":
                        e.wait_ge(sems[op[1]], op[2])
                    elif op[0] == "drain":
                        e.drain()
                    else:
                        _, fn, sem, amt = op
                        inst = fn(e)
                        if sem is not None:
                            inst.then_inc(sems[sem], amt)
            return body

        for eng_name in ENGS:
            if self.ops[eng_name]:
                getattr(block, eng_name)(make(self.ops[eng_name]))


def build_nc():
    nc = bass.Bass("TRN2", target_bir_lowering=False)

    x_d = nc.dram_tensor("x", [NSEQ, T, D], F32, kind="ExternalInput").ap()
    qk_d = nc.dram_tensor("qk", [D, D], F32, kind="ExternalInput").ap()
    vo_d = nc.dram_tensor("vo", [D, D], F32, kind="ExternalInput").ap()
    wt_d = nc.dram_tensor("wt", [D, D], I32, kind="ExternalInput").ap()
    bias_d = nc.dram_tensor("fbias", [1, D], F32, kind="ExternalInput").ap()
    asc_d = nc.dram_tensor("ascale", [1, 1], F32, kind="ExternalInput").ap()
    idf_d = nc.dram_tensor("idf", [128, 128], F32, kind="ExternalInput").ap()
    idb_d = nc.dram_tensor("idb", [128, 128], BF16, kind="ExternalInput").ap()
    ones_d = nc.dram_tensor("onesb", [1, 128], BF16, kind="ExternalInput").ap()
    out_d = nc.dram_tensor("out", [NSEQ, T, D], F32, kind="ExternalOutput").ap()

    qs_hi = nc.dram_tensor("qs_hi", [NSEQ, 128, NCH, T], BF16, kind="Internal").ap()
    qs_lo = nc.dram_tensor("qs_lo", [NSEQ, 128, NCH, T], BF16, kind="Internal").ap()
    vohl_s = nc.dram_tensor("vohl_s", [NCH, 128, 2 * D], BF16, kind="Internal").ap()
    wtb_s = nc.dram_tensor("wtb_s", [NCH, 128, D], BF16, kind="Internal").ap()

    S = Sched()
    es = contextlib.ExitStack()
    E = es.enter_context

    xn_tm_hi = E(nc.sbuf_tensor("xn_tm_hi", [128, NT * D], BF16))
    xn_tm_lo = E(nc.sbuf_tensor("xn_tm_lo", [128, NT * D], BF16))
    xn_fm_hi = E(nc.sbuf_tensor("xn_fm_hi", [128, NCH * T], BF16))
    xn_fm_lo = E(nc.sbuf_tensor("xn_fm_lo", [128, NCH * T], BF16))
    idf_sb = E(nc.sbuf_tensor("idf_sb", [128, 128], F32))
    idb_sb = E(nc.sbuf_tensor("idb_sb", [128, 128], BF16))
    ones_sb = E(nc.sbuf_tensor("ones_sb", [1, 128], BF16))
    bds_hi = E(nc.sbuf_tensor("bds_hi", [1, D], BF16))
    bds_lo = E(nc.sbuf_tensor("bds_lo", [1, D], BF16))
    s_col = E(nc.sbuf_tensor("s_col", [128, 1], F32))
    rs_col = E(nc.sbuf_tensor("rs_col", [128, 1], F32))
    eps_c = E(nc.sbuf_tensor("eps_c", [128, 1], F32))
    msum_c = E(nc.sbuf_tensor("msum_c", [128, 4], F32))
    negmd_c = E(nc.sbuf_tensor("negmd_c", [128, 4], F32))
    ssq_c = E(nc.sbuf_tensor("ssq_c", [128, 4], F32))
    std_c = E(nc.sbuf_tensor("std_c", [128, 4], F32))
    rstd_c = E(nc.sbuf_tensor("rstd_c", [128, 4], F32))
    nmr_c = E(nc.sbuf_tensor("nmr_c", [128, 4], F32))
    rsx_c = E(nc.sbuf_tensor("rsx_c", [128, 4], F32))
    b2_c = E(nc.sbuf_tensor("b2_c", [128, 4], F32))
    nmx_c = E(nc.sbuf_tensor("nmx_c", [128, 4], F32))
    nmx2_c = E(nc.sbuf_tensor("nmx2_c", [128, 4], F32))
    zs_c = E(nc.sbuf_tensor("zs_c", [128, 4], F32))
    rz_c = E(nc.sbuf_tensor("rz_c", [128, 4], F32))

    sems = {}
    for name in ("T", "A", "V", "DS", "DG", "DQ", "DX",
                 "DV0", "DV1", "DV2", "DV3", "DW0", "DW1", "DW2", "DW3"):
        sems[name] = E(nc.semaphore("sem_" + name))

    pro = contextlib.ExitStack()
    P = pro.enter_context
    stg32 = [P(nc.sbuf_tensor(f"p_stg32_{i}", [128, D], F32)) for i in range(2)]
    stgh = [P(nc.sbuf_tensor(f"p_stgh_{i}", [128, D], BF16)) for i in range(2)]
    stgl = [P(nc.sbuf_tensor(f"p_stgl_{i}", [128, D], BF16)) for i in range(2)]
    wint = P(nc.sbuf_tensor("p_wint", [128, D], I32))
    bias_sb = P(nc.sbuf_tensor("bias_sb", [1, D], F32))
    bdsr = P(nc.sbuf_tensor("bdsr", [1, D], F32))
    asc_sb = P(nc.sbuf_tensor("asc_sb", [1, 1], F32))
    s_sb = P(nc.sbuf_tensor("s_sb", [1, 1], F32))
    srecip = P(nc.sbuf_tensor("srecip", [1, 1], F32))
    s_row = P(nc.sbuf_tensor("s_row", [1, 128], F32))
    rs_row = P(nc.sbuf_tensor("rs_row", [1, 128], F32))
    wm1 = P(nc.sbuf_tensor("p_wm1", [128, NCH * D], BF16))
    wstg = P(nc.sbuf_tensor("p_wstg", [128, D], BF16))
    ps_w = P(nc.psum_tensor("p_ps_w", [128, 1024], BF16))
    ps_bc = P(nc.psum_tensor("p_ps_bc", [128, 1], F32))

    with pro:
        S.dma("sync", lambda e: e.dma_start(out=idf_sb[:], in_=idf_d), "DS")
        S.dma("sync", lambda e: e.dma_start(out=idb_sb[:], in_=idb_d), "DS")
        S.dma("sync", lambda e: e.dma_start(out=ones_sb[:], in_=ones_d), "DS")
        S.dma("sync", lambda e: e.dma_start(out=bias_sb[:], in_=bias_d), "DS")
        t_consts = S.dma("sync", lambda e: e.dma_start(out=asc_sb[:], in_=asc_d), "DS")

        S.wait("vector", t_consts)
        S.run("vector", lambda e: e.memset(eps_c[:], 1e-5))
        S.run("vector", lambda e: e.tensor_scalar(
            out=s_sb[:], in0=asc_sb[:], scalar1=1e-5, scalar2=None, op0=ALU.max))
        S.drain("vector")
        S.run("vector", lambda e: e.reciprocal(out=srecip[:], in_=s_sb[:]))
        S.drain("vector")
        t = S.run("vector", lambda e: e.tensor_scalar(
            out=bdsr[:], in0=bias_sb[:], scalar1=srecip[:], scalar2=None,
            op0=ALU.mult), sem="V")
        S.wait("scalar", t)
        th = S.run("scalar", lambda e: e.activation(
            out=bds_hi[:], in_=bdsr[:], func=AF.Identity), sem="A")
        S.wait("vector", th)
        tok_bds = S.run("vector", lambda e: e.scalar_tensor_tensor(
            out=bds_lo[:], in0=bdsr[:], scalar=1.0, in1=bds_hi[:],
            op0=ALU.mult, op1=ALU.subtract), sem="V")
        t = S.run("vector", lambda e: e.tensor_scalar(
            out=s_row[:], in0=ones_sb[:], scalar1=s_sb[:], scalar2=None,
            op0=ALU.mult), sem="V")
        t2 = S.run("vector", lambda e: e.tensor_scalar(
            out=rs_row[:], in0=ones_sb[:], scalar1=srecip[:], scalar2=None,
            op0=ALU.mult), sem="V")
        S.wait("tensor", t2)
        tt = S.run("tensor", lambda e: e.matmul(
            ps_bc[:], s_row[0:1, :], idf_sb[0:1, 0:1], is_transpose=True,
            start=True, stop=True), sem="T")
        S.wait("scalar", tt)
        tok_scol = S.run("scalar", lambda e: e.activation(
            out=s_col[:], in_=ps_bc[:], func=AF.Identity), sem="A")
        S.wait("tensor", tok_scol)
        tt = S.run("tensor", lambda e: e.matmul(
            ps_bc[:], rs_row[0:1, :], idf_sb[0:1, 0:1], is_transpose=True,
            start=True, stop=True), sem="T")
        S.wait("scalar", tt)
        tok_rscol = S.run("scalar", lambda e: e.activation(
            out=rs_col[:], in_=ps_bc[:], func=AF.Identity), sem="A")

        S.wait("sync", tok_bds)
        tok_vo = {}
        for c in range(NCH):
            sl = c % 2
            if c >= 2:
                S.wait("sync", tok_vo[c - 2][0])
            tl = S.dma("sync", lambda e, c=c, sl=sl: e.dma_start(
                out=stg32[sl][:], in_=vo_d[c * 128:(c + 1) * 128, :]), f"DV{sl}")
            S.wait("scalar", tl)
            if c >= 2:
                S.wait("scalar", tok_vo[c - 2][1])
            th = S.run("scalar", lambda e, c=c, sl=sl: e.activation(
                out=stgh[sl][:], in_=stg32[sl][:], func=AF.Identity), sem="A")
            S.wait("vector", th)
            tlo = S.run("vector", lambda e, c=c, sl=sl: e.scalar_tensor_tensor(
                out=stgl[sl][:], in0=stg32[sl][:], scalar=1.0, in1=stgh[sl][:],
                op0=ALU.mult, op1=ALU.subtract), sem="V")
            S.wait("gpsimd", th)
            d1 = S.dma("gpsimd", lambda e, c=c, sl=sl: e.dma_start(
                out=vohl_s[c, :, 0:D], in_=stgh[sl][:]), "DG")
            S.wait("gpsimd", tlo)
            d2 = S.dma("gpsimd", lambda e, c=c, sl=sl: e.dma_start(
                out=vohl_s[c, :, D:2 * D], in_=stgl[sl][:]), "DG")
            tok_vo[c] = (tlo, d1, d2)

        tok_wc = {}
        for oc in range(NCH):
            if oc >= 1:
                S.wait("sync", tok_wc[oc - 1])
            tl = S.dma("sync", lambda e, oc=oc: e.dma_start(
                out=wint[:], in_=wt_d[oc * 128:(oc + 1) * 128, :]), "DS")
            S.wait("vector", tl)
            tok_wc[oc] = S.run("vector", lambda e, oc=oc: e.tensor_scalar(
                out=wm1[:, oc * D:(oc + 1) * D], in0=wint[:], scalar1=-1.0,
                scalar2=None, op0=ALU.add), sem="V")
        prev_ev = None
        prev_dma = None
        S.wait("tensor", tok_wc[NCH - 1])
        for ic in range(NCH):
            if prev_ev is not None:
                S.wait("tensor", prev_ev)
            ttr = None
            for oc in range(NCH):
                ttr = S.run("tensor", lambda e, ic=ic, oc=oc: e.matmul(
                    ps_w[:, oc * 128:(oc + 1) * 128],
                    wm1[:, oc * D + ic * 128: oc * D + (ic + 1) * 128],
                    idb_sb[:], is_transpose=True, start=(oc == 0),
                    stop=(oc == NCH - 1)), sem="T" if oc == NCH - 1 else None)
            S.wait("scalar", ttr)
            if prev_dma is not None:
                S.wait("scalar", prev_dma)
            prev_ev = S.run("scalar", lambda e: e.activation(
                out=wstg[:], in_=ps_w[:], func=AF.Identity), sem="A")
            S.wait("gpsimd", prev_ev)
            prev_dma = S.dma("gpsimd", lambda e, ic=ic: e.dma_start(
                out=wtb_s[ic], in_=wstg[:]), "DG")

    DMAS = ("DS", "DG", "DQ", "DX", "DV0", "DV1", "DV2", "DV3", "DW0", "DW1", "DW2", "DW3")
    prev_phase_toks = [S.now(x) for x in (("T", "A", "V") + DMAS)]

    for s in range(NSEQ):
        pa = contextlib.ExitStack()
        P = pa.enter_context
        qk_hi = P(nc.sbuf_tensor(f"a{s}_qk_hi", [128, NCH * D], BF16))
        qk_lo = P(nc.sbuf_tensor(f"a{s}_qk_lo", [128, NCH * D], BF16))
        qkstg = [P(nc.sbuf_tensor(f"a{s}_qkstg{i}", [128, D], F32)) for i in range(2)]
        xt = [P(nc.sbuf_tensor(f"a{s}_xt{i}", [128, D], F32)) for i in range(2)]
        xn32 = [P(nc.sbuf_tensor(f"a{s}_xn32_{i}", [128, D], F32)) for i in range(2)]
        sqsc = P(nc.sbuf_tensor(f"a{s}_sqsc", [128, D], F32))
        qsth = P(nc.sbuf_tensor(f"a{s}_qsth", [128, NCH * 256], BF16))
        qstl = P(nc.sbuf_tensor(f"a{s}_qstl", [128, NCH * 256], BF16))
        ps_trb = [P(nc.psum_tensor(f"a{s}_pstrb{i}", [128, 1024], BF16)) for i in range(2)]
        ps_q = [P(nc.psum_tensor(f"a{s}_psq{i}", [128, 512], F32)) for i in range(2)]

        with pa:
            S.barrier(prev_phase_toks)

            tok_qk = {}
            for c in range(NCH):
                sl = c % 2
                if c >= 2:
                    S.wait("sync", tok_qk[c - 2])
                tl = S.dma("sync", lambda e, c=c, sl=sl: e.dma_start(
                    out=qkstg[sl][:], in_=qk_d[c * 128:(c + 1) * 128, :]), f"DW{sl}")
                S.wait("scalar", tl)
                th = S.run("scalar", lambda e, c=c, sl=sl: e.activation(
                    out=qk_hi[:, c * D:(c + 1) * D], in_=qkstg[sl][:],
                    func=AF.Identity), sem="A")
                S.wait("vector", th)
                tok_qk[c] = S.run("vector", lambda e, c=c, sl=sl: e.scalar_tensor_tensor(
                    out=qk_lo[:, c * D:(c + 1) * D], in0=qkstg[sl][:], scalar=1.0,
                    in1=qk_hi[:, c * D:(c + 1) * D], op0=ALU.mult,
                    op1=ALU.subtract), sem="V")
            tok_qk_done = tok_qk[NCH - 1]

            fmh = xn_fm_hi[:].rearrange("p (c t) -> p c t", c=NCH)
            fml = xn_fm_lo[:].rearrange("p (c t) -> p c t", c=NCH)
            qh_v = qsth[:].rearrange("p (c t) -> p c t", c=NCH)
            ql_v = qstl[:].rearrange("p (c t) -> p c t", c=NCH)

            tok_xn_hi = {}
            tok_xn_lo = {}
            tok_fm_ev = {}
            tok_qev = {}
            tok_qdma = None

            tok_txl = {}
            tok_tms = {}
            tok_txn = {}
            for k in (0, 1):
                tok_txl[k] = S.dma("sync", lambda e, k=k, s=s: e.dma_start(
                    out=xt[k % 2][:], in_=x_d[s, k * 128:(k + 1) * 128, :]),
                    f"DV{k % 2}")
            S.wait("vector", tok_txl[0])
            tok_tms[0] = S.run("vector", lambda e: e.reduce_sum(
                out=msum_c[:, 0:1], in_=xt[0][:], axis=AX.X), sem="V")
            for k in range(NT):
                kb = k % 2
                kx = k % 2
                ci = k % 4
                if k + 1 < NT:
                    S.wait("vector", tok_txl[k + 1])
                    tok_tms[k + 1] = S.run("vector", lambda e, k=k: e.reduce_sum(
                        out=msum_c[:, (k + 1) % 4:(k + 1) % 4 + 1],
                        in_=xt[(k + 1) % 2][:], axis=AX.X), sem="V")
                tms = tok_tms[k]
                S.wait("scalar", tms)
                tnm = S.run("scalar", lambda e, ci=ci: e.activation(
                    out=negmd_c[:, ci:ci + 1], in_=msum_c[:, ci:ci + 1],
                    func=AF.Identity, scale=-1.0 / D), sem="A")
                S.wait("vector", tnm)
                S.run("vector", lambda e, kb=kb, ci=ci, kx=kx: e.tensor_scalar(
                    out=xn32[kx][:], in0=xt[kb][:],
                    scalar1=negmd_c[:, ci:ci + 1], scalar2=None, op0=ALU.add))
                tsq = S.run("vector", lambda e, ci=ci, kx=kx: e.scalar_tensor_tensor(
                    out=sqsc[:], in0=xn32[kx][:], scalar=1.0, in1=xn32[kx][:],
                    op0=ALU.mult, op1=ALU.mult,
                    accum_out=ssq_c[:, ci:ci + 1]), sem="V")
                S.wait("scalar", tsq)
                S.run("scalar", lambda e, ci=ci: e.activation(
                    out=std_c[:, ci:ci + 1], in_=ssq_c[:, ci:ci + 1], func=AF.Ln,
                    scale=1.0 / D, bias=eps_c[:]), sem="A")
                S.drain("scalar")
                trst = S.run("scalar", lambda e, ci=ci: e.activation(
                    out=rstd_c[:, ci:ci + 1], in_=std_c[:, ci:ci + 1], func=AF.Exp,
                    scale=-0.5), sem="A")
                S.wait("vector", trst)
                tnr = S.run("vector", lambda e, ci=ci: e.tensor_scalar(
                    out=nmr_c[:, ci:ci + 1], in0=negmd_c[:, ci:ci + 1],
                    scalar1=rstd_c[:, ci:ci + 1], scalar2=None, op0=ALU.mult),
                    sem="V")
                S.wait("scalar", tnr)
                if k >= 2:
                    S.wait("scalar", tok_xn_hi[k - 2])
                txn = S.run("scalar", lambda e, kb=kb, ci=ci, kx=kx: e.activation(
                    out=xn32[kx][:], in_=xt[kb][:], func=AF.Identity,
                    scale=rstd_c[:, ci:ci + 1], bias=nmr_c[:, ci:ci + 1]), sem="A")
                tok_txn[k] = txn
                if k + 2 < NT:
                    S.wait("sync", txn)
                    tok_txl[k + 2] = S.dma("sync", lambda e, k=k, s=s: e.dma_start(
                        out=xt[k % 2][:], in_=x_d[s, (k + 2) * 128:(k + 3) * 128, :]),
                        f"DV{k % 2}")
                tok_xn_hi[k] = S.run("scalar", lambda e, k=k, kx=kx: e.activation(
                    out=xn_tm_hi[:, k * D:(k + 1) * D], in_=xn32[kx][:],
                    func=AF.Identity), sem="A")
                S.wait("vector", tok_xn_hi[k])
                tok_xn_lo[k] = S.run("vector", lambda e, k=k, kx=kx: e.scalar_tensor_tensor(
                    out=xn_tm_lo[:, k * D:(k + 1) * D], in0=xn32[kx][:], scalar=1.0,
                    in1=xn_tm_hi[:, k * D:(k + 1) * D], op0=ALU.mult,
                    op1=ALU.subtract), sem="V")

            def emit_q(p):
                nonlocal tok_qdma
                S.wait("tensor", tok_qk_done)
                for t in (2 * p, 2 * p + 1):
                    for pl in range(2):
                        S.wait("tensor", tok_fm_ev[(t, pl)])
                for et in range(NCH):
                    eb = et % 2
                    prev = (tok_qev.get((p, et - 2)) if et >= 2
                            else tok_qev.get((p - 1, et + 6)))
                    if prev is not None:
                        S.wait("tensor", prev[1])
                    tq = None
                    passes = ((qk_hi, xn_fm_hi), (qk_lo, xn_fm_hi),
                              (qk_hi, xn_fm_lo))
                    for c in range(NCH):
                        for pi, (lh, rh) in enumerate(passes):
                            last = (c == NCH - 1 and pi == 2)
                            tq = S.run("tensor", lambda e, eb=eb, c=c, lh=lh,
                                       rh=rh, et=et, p=p, first=(c == 0 and pi == 0),
                                       last=last: e.matmul(
                                           ps_q[eb][:, 0:256],
                                           lh[:, c * D + et * 128:
                                              c * D + (et + 1) * 128],
                                           rh[:, c * T + p * 256:
                                              c * T + (p + 1) * 256],
                                           start=first, stop=last),
                                       sem="T" if last else None)
                    S.wait("scalar", tq)
                    if tok_qdma is not None and et == 0:
                        S.wait("scalar", tok_qdma)
                    teh = S.run("scalar", lambda e, eb=eb, et=et: e.activation(
                        out=qsth[:, et * 256:(et + 1) * 256],
                        in_=ps_q[eb][:, 0:256], func=AF.Identity), sem="A")
                    S.wait("vector", teh)
                    tel = S.run("vector", lambda e, eb=eb, et=et:
                                e.scalar_tensor_tensor(
                                    out=qstl[:, et * 256:(et + 1) * 256],
                                    in0=ps_q[eb][:, 0:256], scalar=1.0,
                                    in1=qsth[:, et * 256:(et + 1) * 256],
                                    op0=ALU.mult, op1=ALU.subtract), sem="V")
                    tok_qev[(p, et)] = (teh, tel)
                S.wait("gpsimd", tok_qev[(p, NCH - 1)][1])
                S.dma("gpsimd", lambda e, p=p, s=s: e.dma_start(
                    out=qs_hi[s, :, :, p * 256:(p + 1) * 256], in_=qh_v), "DG")
                tok_qdma = S.dma("gpsimd", lambda e, p=p, s=s: e.dma_start(
                    out=qs_lo[s, :, :, p * 256:(p + 1) * 256], in_=ql_v), "DG")

            for t in range(NT):
                for pl in range(2):
                    tm = xn_tm_hi if pl == 0 else xn_tm_lo
                    fmv = fmh if pl == 0 else fml
                    prev = tok_fm_ev.get((t - 1, pl))
                    if prev is not None:
                        S.wait("tensor", prev)
                    if pl == 0:
                        S.wait("tensor", tok_xn_hi[t])
                    S.wait("tensor", tok_xn_lo[t])
                    ttr = None
                    for c in range(NCH):
                        ttr = S.run("tensor", lambda e, t=t, c=c, pl=pl, tm=tm:
                                    e.matmul(
                            ps_trb[pl][:, c * 128:(c + 1) * 128],
                            tm[:, t * D + c * 128:t * D + (c + 1) * 128],
                            idb_sb[:], is_transpose=True, start=(c == 0),
                            stop=(c == NCH - 1)), sem="T" if c == NCH - 1 else None)
                    dst = fmv[:, :, t * 128:(t + 1) * 128]
                    S.wait("scalar", ttr)
                    tok_fm_ev[(t, pl)] = S.run("scalar", lambda e, pl=pl, dst=dst:
                                               e.activation(
                        out=dst, in_=ps_trb[pl][:], func=AF.Identity), sem="A")
                if t % 2 == 1 and t >= 3:
                    emit_q((t - 3) // 2)

            emit_q(NT // 2 - 1)

            phase_a_toks = [S.now(x) for x in (("T", "A", "V") + DMAS)]

        mn = contextlib.ExitStack()
        P = mn.enter_context
        probs = P(nc.sbuf_tensor(f"m{s}_probs", [128, T], F32))
        q_hi = P(nc.sbuf_tensor(f"m{s}_q_hi", [128, NCH * 128], BF16))
        q_lo = P(nc.sbuf_tensor(f"m{s}_q_lo", [128, NCH * 128], BF16))
        at_hi = P(nc.sbuf_tensor(f"m{s}_at_hi", [128, T], BF16))
        at_lo = P(nc.sbuf_tensor(f"m{s}_at_lo", [128, T], BF16))
        av32 = P(nc.sbuf_tensor(f"m{s}_av32", [128, D], F32))
        avt_hi = P(nc.sbuf_tensor(f"m{s}_avt_hi", [128, D], BF16))
        avt_lo = P(nc.sbuf_tensor(f"m{s}_avt_lo", [128, D], BF16))
        vo_b = [P(nc.sbuf_tensor(f"m{s}_vo_b{i}", [128, 2 * D], BF16)) for i in range(4)]
        wtc = [P(nc.sbuf_tensor(f"m{s}_wtc{i}", [128, D], BF16)) for i in range(4)]
        x_t = P(nc.sbuf_tensor(f"m{s}_x_t", [128, D], F32))
        x2 = [P(nc.sbuf_tensor(f"m{s}_x2_{i}", [128, D], F32)) for i in range(2)]
        rq = av32
        xq32 = [P(nc.sbuf_tensor(f"m{s}_xq32_0", [128, D], F32))] * 2
        xintT = P(nc.sbuf_tensor(f"m{s}_xintT", [128, D], BF16))
        ostg = P(nc.sbuf_tensor(f"m{s}_ostg", [128, D], F32))
        ps_sc = P(nc.psum_tensor(f"m{s}_ps_sc", [128, T], F32))
        ps_av = P(nc.psum_tensor(f"m{s}_ps_av", [128, D], F32))
        ps_at = [P(nc.psum_tensor(f"m{s}_ps_at{i}", [128, 512], F32))
                 for i in range(2)]

        with mn:
            S.barrier(phase_a_toks)

            qv_hi = q_hi[:].rearrange("p (c t) -> p c t", c=NCH)
            qv_lo = q_lo[:].rearrange("p (c t) -> p c t", c=NCH)

            tok_exp = {}
            tok_at_ev = {}
            tok_at_g = {}
            tok_sc = {}
            tok_av_h = {}
            tok_av_ev = {}
            tok_avt_g = {}
            tok_avt_ev = {}
            tok_out_c = {}
            tok_out_ev = {}
            tok_xt = {}
            tok_xt_ev = {}
            tok_ff_c = {}
            tok_ff_ev = {}
            tok_xint = {}
            tok_rz = {}
            tok_store = {}
            tok_vold = {}
            tok_wtd = {}
            tok_qld = {}
            tok_xld = {}
            tok_tz = {}

            for k in range(NT + 2):
                j1, j2 = k - 1, k - 2
                ci = k % 4

                if 0 <= j1 < NT:
                    for c in (0, 1, 2, 3):
                        S.wait("sync", tok_out_c.get((j1 - 1, c + 4)))
                        tok_vold[(j1, c)] = S.dma("sync", lambda e, c=c: e.dma_start(
                            out=vo_b[c % 4][:], in_=vohl_s[c]), f"DV{c % 4}")
                if 0 <= j2 < NT:
                    for c in (0, 1, 2, 3):
                        S.wait("sync", tok_ff_c.get((j2 - 1, c + 4)))
                        tok_wtd[(j2, c)] = S.dma("scalar", lambda e, c=c: e.dma_start(
                            out=wtc[c % 4][:], in_=wtb_s[c]), f"DW{c % 4}")

                if k < NT:
                    if k == 0:
                        S.wait("sync", phase_a_toks[4])
                        S.dma("scalar", lambda e, s=s: e.dma_start(
                            out=qv_hi, in_=qs_hi[s, :, :, 0:128]), "DQ")
                        tok_qld[0] = S.dma("scalar", lambda e, s=s: e.dma_start(
                            out=qv_lo, in_=qs_lo[s, :, :, 0:128]), "DQ")
                    S.wait("tensor", tok_qld[k])
                    S.wait("tensor", tok_exp.get(k - 1))
                    if k - 1 >= 0:
                        S.wait("tensor", tok_at_ev[(k - 1, 2)][0])
                        S.wait("tensor", tok_at_ev[(k - 1, 2)][1])
                        S.wait("tensor", tok_at_ev[(k - 1, 3)][0])
                        S.wait("tensor", tok_at_ev[(k - 1, 3)][1])
                    S.wait("tensor", tok_xt_ev.get(k - 3))
                    S.wait("tensor", tok_ff_ev.get(k - 3))
                    t = None
                    passes = ((q_hi, xn_fm_hi), (q_lo, xn_fm_hi), (q_hi, xn_fm_lo))
                    for g in range(4):
                        for c in range(NCH):
                            for pi, (lh, rh) in enumerate(passes):
                                last = (g == 3 and c == NCH - 1 and pi == 2)
                                t = S.run("tensor", lambda e, g=g, c=c, lh=lh, rh=rh,
                                          first=(c == 0 and pi == 0),
                                          stop=(c == NCH - 1 and pi == 2): e.matmul(
                                              ps_sc[:, g * 512:(g + 1) * 512],
                                              lh[:, c * 128:(c + 1) * 128],
                                              rh[:, c * T + g * 512:
                                                 c * T + (g + 1) * 512],
                                              start=first, stop=stop),
                                          sem="T" if last else None)
                    tok_sc[k] = t
                    if k + 1 < NT:
                        S.wait("scalar", tok_sc[k])
                        S.dma("scalar", lambda e, k=k, s=s: e.dma_start(
                            out=qv_hi, in_=qs_hi[s, :, :, (k + 1) * 128:(k + 2) * 128]),
                            "DQ")
                        tok_qld[k + 1] = S.dma("scalar", lambda e, k=k, s=s: e.dma_start(
                            out=qv_lo, in_=qs_lo[s, :, :, (k + 1) * 128:(k + 2) * 128]),
                            "DQ")

                    S.wait("vector", tok_sc[k])
                    S.run("vector", lambda e, ci=ci: e.reduce_max(
                        out=nmx_c[:, ci:ci + 1], in_=ps_sc[:], axis=AX.X,
                        negate=True), sem="V")
                    S.drain("vector")
                    tmx2 = S.run("vector", lambda e, ci=ci: e.tensor_scalar(
                        out=nmx2_c[:, ci:ci + 1], in0=nmx_c[:, ci:ci + 1],
                        scalar1=1.0 / 32.0, scalar2=None, op0=ALU.mult), sem="V")
                    S.wait("scalar", tmx2)
                    if k >= 1:
                        S.wait("scalar", tok_at_g[(k - 1, 3)])
                    S.wait("scalar", tok_xint.get(k - 2))
                    tok_exp[k] = S.run("scalar", lambda e, ci=ci: e.activation(
                        out=probs[:], in_=ps_sc[:], func=AF.Exp,
                        bias=nmx2_c[:, ci:ci + 1], scale=1.0 / 32.0,
                        accum_out=zs_c[:, ci:ci + 1]), sem="A")
                    S.wait("vector", tok_exp[k])
                    tok_rz[k] = S.run("vector", lambda e, ci=ci: e.reciprocal(
                        out=rz_c[:, ci:ci + 1], in_=zs_c[:, ci:ci + 1]), sem="V")

                if 0 <= j1 < NT:
                    for g in range(4):
                        S.wait("tensor", tok_at_ev[(j1, g)][0])
                        S.wait("tensor", tok_at_ev[(j1, g)][1])
                    S.wait("tensor", tok_out_ev.get(j1 - 1))
                    passes = ((at_hi, xn_tm_hi), (at_lo, xn_tm_hi), (at_hi, xn_tm_lo))
                    for h in range(2):
                        t = None
                        for j in range(NT):
                            for pi, (lh, rh) in enumerate(passes):
                                last = (j == NT - 1 and pi == 2)
                                t = S.run("tensor", lambda e, h=h, j=j, lh=lh, rh=rh,
                                          first=(j == 0 and pi == 0), last=last:
                                          e.matmul(
                                              ps_av[:, h * 512:(h + 1) * 512],
                                              lh[:, j * 128:(j + 1) * 128],
                                              rh[:, j * D + h * 512:
                                                 j * D + (h + 1) * 512],
                                              start=first, stop=last),
                                          sem="T" if last else None)
                        tok_av_h[(j1, h)] = t
                        S.wait("scalar", t)
                        S.wait("scalar", tok_xint.get(j1 - 1))
                        tok_av_ev[(j1, h)] = S.run("scalar", lambda e, h=h:
                                                   e.activation(
                            out=av32[:, h * 512:(h + 1) * 512],
                            in_=ps_av[:, h * 512:(h + 1) * 512],
                            func=AF.Identity), sem="A")

                if 0 <= j1 < NT:
                    for g in range(2):
                        S.wait("tensor", tok_av_ev[(j1, g)])
                        S.wait("tensor", tok_at_ev[(k - 1, g)][0])
                        S.wait("tensor", tok_at_ev[(k - 1, g)][1])
                        t = None
                        for j in range(4):
                            c = g * 4 + j
                            t = S.run("tensor", lambda e, g=g, j=j, c=c: e.matmul(
                                ps_at[g][:, j * 128:(j + 1) * 128],
                                av32[:, c * 128:(c + 1) * 128], idf_sb[:],
                                is_transpose=True, start=(j == 0), stop=(j == 3)),
                                sem="T" if j == 3 else None)
                        tok_avt_g[(j1, g)] = t
                        S.wait("scalar", t)
                        teh = S.run("scalar", lambda e, g=g: e.activation(
                            out=avt_hi[:, g * 512:(g + 1) * 512], in_=ps_at[g][:],
                            func=AF.Identity), sem="A")
                        S.wait("vector", teh)
                        tel = S.run("vector", lambda e, g=g: e.scalar_tensor_tensor(
                            out=avt_lo[:, g * 512:(g + 1) * 512], in0=ps_at[g][:],
                            scalar=1.0, in1=avt_hi[:, g * 512:(g + 1) * 512],
                            op0=ALU.mult, op1=ALU.subtract), sem="V")
                        tok_avt_ev[(j1, g)] = (teh, tel)

                if 0 <= j1 < NT:
                    for c in range(NCH):
                        gi = 0 if c < 4 else 1
                        S.wait("tensor", tok_avt_ev[(j1, gi)][0])
                        S.wait("tensor", tok_avt_ev[(j1, gi)][1])
                        S.wait("tensor", tok_vold[(j1, c)])
                        vhi = vo_b[c % 4][:, 0:D]
                        vlo = vo_b[c % 4][:, D:2 * D]
                        passes = ((avt_hi, vhi), (avt_lo, vhi), (avt_hi, vlo))
                        t = None
                        for h in range(2):
                            for pi, (lh, rh) in enumerate(passes):
                                t = S.run("tensor", lambda e, h=h, c=c, lh=lh, rh=rh,
                                          first=(c == 0 and pi == 0),
                                          stop=(c == NCH - 1 and pi == 2): e.matmul(
                                              ps_av[:, h * 512:(h + 1) * 512],
                                              lh[:, c * 128:(c + 1) * 128],
                                              rh[:, h * 512:(h + 1) * 512],
                                              start=first, stop=stop),
                                          sem="T" if (h == 1 and pi == 2) else None)
                        tok_out_c[(j1, c)] = t
                        if c + 4 < NCH:
                            S.wait("sync", t)
                            tok_vold[(j1, c + 4)] = S.dma("sync", lambda e, c=c:
                                                          e.dma_start(
                                out=vo_b[c % 4][:], in_=vohl_s[c + 4]), f"DV{c % 4}")

                    cj = j1 % 4
                    S.wait("vector", tok_out_c[(j1, NCH - 1)])
                    S.wait("vector", tok_xld[j1])
                    S.wait("vector", tok_rz[j1])
                    tok_out_ev[j1] = S.run("vector", lambda e, j1=j1, cj=cj:
                                           e.scalar_tensor_tensor(
                        out=x2[j1 % 2][:], in0=ps_av[:],
                        scalar=rz_c[:, cj:cj + 1], in1=x_t[:],
                        op0=ALU.mult, op1=ALU.add), sem="V")

                    if j1 + 1 < NT:
                        S.wait("scalar", tok_out_ev[j1])
                        tok_xld[j1 + 1] = S.dma("scalar", lambda e, j1=j1, s=s: e.dma_start(
                            out=x_t[:], in_=x_d[s, (j1 + 1) * 128:(j1 + 2) * 128, :]),
                            "DX")
                if k == 0:
                    tok_xld[0] = S.dma("scalar", lambda e, s=s: e.dma_start(
                        out=x_t[:], in_=x_d[s, 0:128, :]), "DX")

                if k < NT:
                    S.wait("tensor", tok_exp[k])
                    for g in range(4):
                        if g < 2 and 0 <= j1 < NT:
                            S.wait("tensor", tok_avt_ev[(j1, g)][0])
                            S.wait("tensor", tok_avt_ev[(j1, g)][1])
                        pdst = (ps_at[g] if g < 2 else None)
                        t = None
                        for j in range(4):
                            c = g * 4 + j
                            dst = (ps_at[g][:, j * 128:(j + 1) * 128] if g < 2 else
                                   ps_sc[:, (g - 2) * 512 + j * 128:
                                         (g - 2) * 512 + (j + 1) * 128])
                            t = S.run("tensor", lambda e, dst=dst, c=c: e.matmul(
                                dst, probs[:, c * 128:(c + 1) * 128], idf_sb[:],
                                is_transpose=True, start=(c % 4 == 0),
                                stop=(c % 4 == 3)), sem="T" if j == 3 else None)
                        tok_at_g[(k, g)] = t
                        S.wait("scalar", t)
                        psrc = (ps_at[g][:] if g < 2 else
                                ps_sc[:, (g - 2) * 512:(g - 1) * 512])
                        teh = S.run("scalar", lambda e, g=g, psrc=psrc: e.activation(
                            out=at_hi[:, g * 512:(g + 1) * 512], in_=psrc,
                            func=AF.Identity), sem="A")
                        S.wait("vector", teh)
                        tel = S.run("vector", lambda e, g=g, psrc=psrc:
                                    e.scalar_tensor_tensor(
                            out=at_lo[:, g * 512:(g + 1) * 512], in0=psrc,
                            scalar=1.0, in1=at_hi[:, g * 512:(g + 1) * 512],
                            op0=ALU.mult, op1=ALU.subtract), sem="V")
                        tok_at_ev[(k, g)] = (teh, tel)

                if 0 <= j2 < NT:
                    kk = min(k, NT - 1)
                    S.wait("tensor", tok_xint[j2])
                    tev = []
                    for g in range(2):
                        S.wait("tensor", tok_at_ev[(kk, 2 + g)][0])
                        S.wait("tensor", tok_at_ev[(kk, 2 + g)][1])
                        t = None
                        for j in range(4):
                            c = g * 4 + j
                            t = S.run("tensor", lambda e, g=g, j=j, c=c, j2=j2:
                                      e.matmul(
                                          ps_sc[:, g * 512 + j * 128:
                                                g * 512 + (j + 1) * 128],
                                          xq32[j2 % 2][:, c * 128:(c + 1) * 128],
                                          idf_sb[:], is_transpose=True,
                                          start=(j == 0), stop=(j == 3)),
                                      sem="T" if j == 3 else None)
                        S.wait("scalar", t)
                        tev.append(S.run("scalar", lambda e, g=g: e.activation(
                            out=xintT[:, g * 512:(g + 1) * 512],
                            in_=ps_sc[:, g * 512:(g + 1) * 512],
                            func=AF.Identity), sem="A"))
                    tok_xt_ev[j2] = tev[-1]

                if 0 <= j2 < NT:
                    S.wait("tensor", tok_xt_ev[j2])
                    S.wait("tensor", tok_bds)
                    for h in range(2):
                        S.run("tensor", lambda e, h=h: e.matmul(
                            ps_sc[:, 1024 + h * 512:1024 + (h + 1) * 512],
                            ones_sb[0:1, :], bds_hi[0:1, h * 512:(h + 1) * 512],
                            start=True, stop=False))
                        S.run("tensor", lambda e, h=h: e.matmul(
                            ps_sc[:, 1024 + h * 512:1024 + (h + 1) * 512],
                            ones_sb[0:1, :], bds_lo[0:1, h * 512:(h + 1) * 512],
                            start=False, stop=False))
                    for c in range(NCH):
                        S.wait("tensor", tok_wtd[(j2, c)])
                        t = None
                        for h in range(2):
                            t = S.run("tensor", lambda e, h=h, c=c, j2=j2: e.matmul(
                                ps_sc[:, 1024 + h * 512:1024 + (h + 1) * 512],
                                xintT[:, c * 128:(c + 1) * 128],
                                wtc[c % 4][:, h * 512:(h + 1) * 512],
                                start=False, stop=(c == NCH - 1)),
                                sem="T" if h == 1 else None)
                        tok_ff_c[(j2, c)] = t
                        if c + 4 < NCH:
                            S.wait("scalar", t)
                            tok_wtd[(j2, c + 4)] = S.dma("scalar", lambda e, c=c:
                                                         e.dma_start(
                                out=wtc[c % 4][:], in_=wtb_s[c + 4]), f"DW{c % 4}")

                if 0 <= j1 < NT:
                    cj = j1 % 4
                    S.wait("vector", tok_at_g.get((min(k, NT - 1), 3)))
                    tms = S.run("vector", lambda e, j1=j1, cj=cj: e.reduce_sum(
                        out=msum_c[:, cj:cj + 1], in_=x2[j1 % 2][:], axis=AX.X),
                        sem="V")
                    S.wait("scalar", tms)
                    tnm = S.run("scalar", lambda e, cj=cj: e.activation(
                        out=negmd_c[:, cj:cj + 1], in_=msum_c[:, cj:cj + 1],
                        func=AF.Identity, scale=-1.0 / D), sem="A")
                    S.wait("vector", tnm)
                    S.wait("vector", tok_xint.get(j1 - 1))
                    S.run("vector", lambda e, j1=j1, cj=cj: e.tensor_scalar(
                        out=probs[:, 0:1024], in0=x2[j1 % 2][:],
                        scalar1=negmd_c[:, cj:cj + 1], scalar2=None, op0=ALU.add))
                    tsq = S.run("vector", lambda e, cj=cj: e.scalar_tensor_tensor(
                        out=rq[:], in0=probs[:, 0:1024], scalar=1.0, in1=probs[:, 0:1024],
                        op0=ALU.mult, op1=ALU.mult,
                        accum_out=ssq_c[:, cj:cj + 1]), sem="V")
                    S.wait("scalar", tsq)
                    S.run("scalar", lambda e, cj=cj: e.activation(
                        out=std_c[:, cj:cj + 1], in_=ssq_c[:, cj:cj + 1],
                        func=AF.Ln, scale=1.0 / D, bias=eps_c[:]), sem="A")
                    S.drain("scalar")
                    trst = S.run("scalar", lambda e, cj=cj: e.activation(
                        out=rstd_c[:, cj:cj + 1], in_=std_c[:, cj:cj + 1],
                        func=AF.Exp, scale=-0.5), sem="A")
                    S.wait("vector", trst)
                    S.wait("vector", tok_rscol)
                    S.run("vector", lambda e, cj=cj: e.tensor_scalar(
                        out=rsx_c[:, cj:cj + 1], in0=rstd_c[:, cj:cj + 1],
                        scalar1=rs_col[:], scalar2=None, op0=ALU.mult))
                    S.drain("vector")
                    tb2 = S.run("vector", lambda e, cj=cj: e.tensor_scalar(
                        out=b2_c[:, cj:cj + 1], in0=negmd_c[:, cj:cj + 1],
                        scalar1=rsx_c[:, cj:cj + 1], scalar2=128.5, op0=ALU.mult,
                        op1=ALU.add), sem="V")
                    S.wait("scalar", tb2)
                    tz = S.run("scalar", lambda e, j1=j1, cj=cj: e.activation(
                        out=probs[:, 0:1024], in_=x2[j1 % 2][:], func=AF.Identity,
                        scale=rsx_c[:, cj:cj + 1], bias=b2_c[:, cj:cj + 1]), sem="A")
                    tok_tz[j1] = tz


                if 0 <= j2 < NT:
                    S.wait("vector", tok_ff_c[(j2, NCH - 1)])
                    S.wait("vector", tok_scol)
                    S.wait("vector", tok_store.get(j2 - 1))
                    tok_ff_ev[j2] = S.run("vector", lambda e, j2=j2:
                                          e.scalar_tensor_tensor(
                        out=ostg[:], in0=ps_sc[:, 1024:2048], scalar=s_col[:],
                        in1=x2[j2 % 2][:], op0=ALU.mult, op1=ALU.add), sem="V")
                    S.wait("gpsimd", tok_ff_ev[j2])
                    tok_store[j2] = S.dma("gpsimd", lambda e, j2=j2, s=s: e.dma_start(
                        out=out_d[s, j2 * 128:(j2 + 1) * 128, :], in_=ostg[:]), "DG")

                if 0 <= j1 < NT:
                    S.wait("vector", tok_tz[j1])
                    S.wait("vector", tok_store.get(j1 - 1))
                    S.run("vector", lambda e: e.tensor_scalar(
                        out=probs[:, 0:1024], in0=probs[:, 0:1024], scalar1=0.9, scalar2=255.9,
                        op0=ALU.max, op1=ALU.min), sem="V")
                    S.run("vector", lambda e: e.tensor_copy(out=rq[:], in_=probs[:, 0:1024]),
                          sem="V")
                    for b in (128.0, 64.0, 32.0, 16.0, 8.0, 4.0, 2.0, 1.0):
                        S.run("vector", lambda e, b=b: e.tensor_scalar(
                            out=ostg[:], in0=rq[:], scalar1=b, scalar2=b,
                            op0=ALU.is_ge, op1=ALU.mult))
                        S.run("vector", lambda e: e.tensor_tensor(
                            out=rq[:], in0=rq[:], in1=ostg[:], op=ALU.subtract))
                    S.run("vector", lambda e: e.tensor_tensor(
                        out=ostg[:], in0=probs[:, 0:1024], in1=rq[:], op=ALU.subtract), sem="V")
                    S.wait("vector", tok_xt_ev.get(j1 - 1))
                    tok_xint[j1] = S.run("vector", lambda e, j1=j1: e.tensor_scalar(
                        out=xq32[j1 % 2][:], in0=ostg[:], scalar1=-128.0,
                        scalar2=None, op0=ALU.add), sem="V")


            prev_phase_toks = [S.now(x) for x in (("T", "A", "V") + DMAS)]

    S.wait("gpsimd", S.now("DG"))

    with es:
        with nc.Block() as block:
            S.realize(block, sems)
    return nc


_NC_CACHE = []


def kernel(L0_attn_x, L0_attn_QK, L0_attn_VO, L0_ffn_ternary_weight,
           L0_ffn_bias, L0_ffn_act_scale):
    x = np.ascontiguousarray(np.asarray(L0_attn_x, dtype=np.float32))
    qk = np.ascontiguousarray(np.asarray(L0_attn_QK, dtype=np.float32))
    vo = np.ascontiguousarray(np.asarray(L0_attn_VO, dtype=np.float32))
    wt = np.ascontiguousarray(np.asarray(L0_ffn_ternary_weight, dtype=np.int32))
    fb = np.ascontiguousarray(np.asarray(L0_ffn_bias, dtype=np.float32).reshape(1, D))
    asc = np.ascontiguousarray(np.asarray(L0_ffn_act_scale,
                                          dtype=np.float32).reshape(1, 1))
    idf = np.eye(128, dtype=np.float32)
    idb = np.eye(128, dtype=BF)
    onesb = np.ones((1, 128), dtype=BF)

    shards = x.reshape(8, NSEQ, T, D)
    in_maps = [{"x": shards[c], "qk": qk, "vo": vo, "wt": wt, "fbias": fb,
                "ascale": asc, "idf": idf, "idb": idb, "onesb": onesb}
               for c in range(8)]
    if not _NC_CACHE:
        _NC_CACHE.append(build_nc())
    nc = _NC_CACHE[0]
    res = run_bass_kernel_spmd(nc, in_maps, core_ids=list(range(8)))
    out = np.stack([np.asarray(r["out"]) for r in res.results], axis=0)
    return out.reshape(16, T, D)


if __name__ == "__main__":
    nc = build_nc()
    print("built", sum(1 for _ in nc.all_instructions()), "instructions")


# version 10
# speedup vs baseline: 1.0170x; 1.0170x over previous
import contextlib
import numpy as np
import ml_dtypes
import concourse.bass as bass
import concourse.mybir as mybir
from concourse.bass_utils import run_bass_kernel_spmd

F32 = mybir.dt.float32
BF16 = mybir.dt.bfloat16
I32 = mybir.dt.int32
AF = mybir.ActivationFunctionType
ALU = mybir.AluOpType
AX = mybir.AxisListType
BF = ml_dtypes.bfloat16

NSEQ = 2
NT = 16
NCH = 8
D = 1024
T = 2048

ENGS = ("tensor", "scalar", "vector", "sync", "gpsimd")


class Sched:

    def __init__(self):
        self.ops = {e: [] for e in ENGS}
        self.cnt = {}

    def run(self, eng, fn, sem=None, amt=1):
        self.ops[eng].append(("inst", fn, sem, amt))
        if sem is not None:
            self.cnt[sem] = self.cnt.get(sem, 0) + amt
            return (sem, self.cnt[sem])
        return None

    def dma(self, eng, fn, sem):
        return self.run(eng, fn, sem=sem, amt=16)

    def wait(self, eng, tok):
        if tok is None:
            return
        sem, val = tok
        if val <= 0:
            return
        assert self.cnt.get(sem, 0) >= val, f"wait on future token {sem}={val}"
        ops = self.ops[eng]
        if ops and ops[-1][0] == "wait" and ops[-1][1] == sem:
            ops[-1] = ("wait", sem, max(ops[-1][2], val))
        else:
            ops.append(("wait", sem, val))

    def drain(self, eng):
        self.ops[eng].append(("drain",))

    def barrier(self, toks):
        for e in ENGS:
            for tok in toks:
                self.wait(e, tok)

    def now(self, sem):
        return (sem, self.cnt.get(sem, 0))

    def realize(self, block, sems):
        def make(oplist):
            def body(e):
                for op in oplist:
                    if op[0] == "wait":
                        e.wait_ge(sems[op[1]], op[2])
                    elif op[0] == "drain":
                        e.drain()
                    else:
                        _, fn, sem, amt = op
                        inst = fn(e)
                        if sem is not None:
                            inst.then_inc(sems[sem], amt)
            return body

        for eng_name in ENGS:
            if self.ops[eng_name]:
                getattr(block, eng_name)(make(self.ops[eng_name]))


def build_nc():
    nc = bass.Bass("TRN2", target_bir_lowering=False)

    x_d = nc.dram_tensor("x", [NSEQ, T, D], F32, kind="ExternalInput").ap()
    qk_d = nc.dram_tensor("qk", [D, D], F32, kind="ExternalInput").ap()
    vo_d = nc.dram_tensor("vo", [D, D], F32, kind="ExternalInput").ap()
    wt_d = nc.dram_tensor("wt", [D, D], I32, kind="ExternalInput").ap()
    bias_d = nc.dram_tensor("fbias", [1, D], F32, kind="ExternalInput").ap()
    asc_d = nc.dram_tensor("ascale", [1, 1], F32, kind="ExternalInput").ap()
    idf_d = nc.dram_tensor("idf", [128, 128], F32, kind="ExternalInput").ap()
    idb_d = nc.dram_tensor("idb", [128, 128], BF16, kind="ExternalInput").ap()
    ones_d = nc.dram_tensor("onesb", [1, 128], BF16, kind="ExternalInput").ap()
    out_d = nc.dram_tensor("out", [NSEQ, T, D], F32, kind="ExternalOutput").ap()

    qs_hi = nc.dram_tensor("qs_hi", [NSEQ, 128, NCH, T], BF16, kind="Internal").ap()
    qs_lo = nc.dram_tensor("qs_lo", [NSEQ, 128, NCH, T], BF16, kind="Internal").ap()
    voh_s = nc.dram_tensor("voh_s", [NCH, 128, D], BF16, kind="Internal").ap()
    vol_s = nc.dram_tensor("vol_s", [NCH, 128, D], BF16, kind="Internal").ap()
    wtb_s = nc.dram_tensor("wtb_s", [NCH, 128, D], BF16, kind="Internal").ap()

    S = Sched()
    es = contextlib.ExitStack()
    E = es.enter_context

    xn_tm_hi = E(nc.sbuf_tensor("xn_tm_hi", [128, NT * D], BF16))
    xn_tm_lo = E(nc.sbuf_tensor("xn_tm_lo", [128, NT * D], BF16))
    xn_fm_hi = E(nc.sbuf_tensor("xn_fm_hi", [128, NCH * T], BF16))
    xn_fm_lo = E(nc.sbuf_tensor("xn_fm_lo", [128, NCH * T], BF16))
    idf_sb = E(nc.sbuf_tensor("idf_sb", [128, 128], F32))
    idb_sb = E(nc.sbuf_tensor("idb_sb", [128, 128], BF16))
    ones_sb = E(nc.sbuf_tensor("ones_sb", [1, 128], BF16))
    bds_hi = E(nc.sbuf_tensor("bds_hi", [1, D], BF16))
    bds_lo = E(nc.sbuf_tensor("bds_lo", [1, D], BF16))
    s_col = E(nc.sbuf_tensor("s_col", [128, 1], F32))
    rs_col = E(nc.sbuf_tensor("rs_col", [128, 1], F32))
    eps_c = E(nc.sbuf_tensor("eps_c", [128, 1], F32))
    msum_c = E(nc.sbuf_tensor("msum_c", [128, 4], F32))
    negmd_c = E(nc.sbuf_tensor("negmd_c", [128, 4], F32))
    ssq_c = E(nc.sbuf_tensor("ssq_c", [128, 4], F32))
    std_c = E(nc.sbuf_tensor("std_c", [128, 4], F32))
    rstd_c = E(nc.sbuf_tensor("rstd_c", [128, 4], F32))
    nmr_c = E(nc.sbuf_tensor("nmr_c", [128, 4], F32))
    rsx_c = E(nc.sbuf_tensor("rsx_c", [128, 4], F32))
    b2_c = E(nc.sbuf_tensor("b2_c", [128, 4], F32))
    nmx_c = E(nc.sbuf_tensor("nmx_c", [128, 4], F32))
    nmx2_c = E(nc.sbuf_tensor("nmx2_c", [128, 4], F32))
    zs_c = E(nc.sbuf_tensor("zs_c", [128, 4], F32))
    rz_c = E(nc.sbuf_tensor("rz_c", [128, 4], F32))

    sems = {}
    for name in ("T", "A", "V", "DS", "DG", "DQ", "DX",
                 "DV0", "DV1", "DV2", "DW0", "DW1", "DW2", "DW3"):
        sems[name] = E(nc.semaphore("sem_" + name))

    pro = contextlib.ExitStack()
    P = pro.enter_context
    stg32 = [P(nc.sbuf_tensor(f"p_stg32_{i}", [128, D], F32)) for i in range(2)]
    stgh = [P(nc.sbuf_tensor(f"p_stgh_{i}", [128, D], BF16)) for i in range(2)]
    stgl = [P(nc.sbuf_tensor(f"p_stgl_{i}", [128, D], BF16)) for i in range(2)]
    wint = P(nc.sbuf_tensor("p_wint", [128, D], I32))
    bias_sb = P(nc.sbuf_tensor("bias_sb", [1, D], F32))
    bdsr = P(nc.sbuf_tensor("bdsr", [1, D], F32))
    asc_sb = P(nc.sbuf_tensor("asc_sb", [1, 1], F32))
    s_sb = P(nc.sbuf_tensor("s_sb", [1, 1], F32))
    srecip = P(nc.sbuf_tensor("srecip", [1, 1], F32))
    s_row = P(nc.sbuf_tensor("s_row", [1, 128], F32))
    rs_row = P(nc.sbuf_tensor("rs_row", [1, 128], F32))
    wm1 = P(nc.sbuf_tensor("p_wm1", [128, NCH * D], BF16))
    wstg = P(nc.sbuf_tensor("p_wstg", [128, D], BF16))
    ps_w = P(nc.psum_tensor("p_ps_w", [128, 1024], BF16))
    ps_bc = P(nc.psum_tensor("p_ps_bc", [128, 1], F32))

    with pro:
        S.dma("sync", lambda e: e.dma_start(out=idf_sb[:], in_=idf_d), "DS")
        S.dma("sync", lambda e: e.dma_start(out=idb_sb[:], in_=idb_d), "DS")
        S.dma("sync", lambda e: e.dma_start(out=ones_sb[:], in_=ones_d), "DS")
        S.dma("sync", lambda e: e.dma_start(out=bias_sb[:], in_=bias_d), "DS")
        t_consts = S.dma("sync", lambda e: e.dma_start(out=asc_sb[:], in_=asc_d), "DS")

        S.wait("vector", t_consts)
        S.run("vector", lambda e: e.memset(eps_c[:], 1e-5))
        S.run("vector", lambda e: e.tensor_scalar(
            out=s_sb[:], in0=asc_sb[:], scalar1=1e-5, scalar2=None, op0=ALU.max))
        S.drain("vector")
        S.run("vector", lambda e: e.reciprocal(out=srecip[:], in_=s_sb[:]))
        S.drain("vector")
        t = S.run("vector", lambda e: e.tensor_scalar(
            out=bdsr[:], in0=bias_sb[:], scalar1=srecip[:], scalar2=None,
            op0=ALU.mult), sem="V")
        S.wait("scalar", t)
        th = S.run("scalar", lambda e: e.activation(
            out=bds_hi[:], in_=bdsr[:], func=AF.Identity), sem="A")
        S.wait("vector", th)
        tok_bds = S.run("vector", lambda e: e.scalar_tensor_tensor(
            out=bds_lo[:], in0=bdsr[:], scalar=1.0, in1=bds_hi[:],
            op0=ALU.mult, op1=ALU.subtract), sem="V")
        t = S.run("vector", lambda e: e.tensor_scalar(
            out=s_row[:], in0=ones_sb[:], scalar1=s_sb[:], scalar2=None,
            op0=ALU.mult), sem="V")
        t2 = S.run("vector", lambda e: e.tensor_scalar(
            out=rs_row[:], in0=ones_sb[:], scalar1=srecip[:], scalar2=None,
            op0=ALU.mult), sem="V")
        S.wait("tensor", t2)
        tt = S.run("tensor", lambda e: e.matmul(
            ps_bc[:], s_row[0:1, :], idf_sb[0:1, 0:1], is_transpose=True,
            start=True, stop=True), sem="T")
        S.wait("scalar", tt)
        tok_scol = S.run("scalar", lambda e: e.activation(
            out=s_col[:], in_=ps_bc[:], func=AF.Identity), sem="A")
        S.wait("tensor", tok_scol)
        tt = S.run("tensor", lambda e: e.matmul(
            ps_bc[:], rs_row[0:1, :], idf_sb[0:1, 0:1], is_transpose=True,
            start=True, stop=True), sem="T")
        S.wait("scalar", tt)
        tok_rscol = S.run("scalar", lambda e: e.activation(
            out=rs_col[:], in_=ps_bc[:], func=AF.Identity), sem="A")

        S.wait("sync", tok_bds)
        tok_vo = {}
        for c in range(NCH):
            sl = c % 2
            if c >= 2:
                S.wait("sync", tok_vo[c - 2][0])
            tl = S.dma("sync", lambda e, c=c, sl=sl: e.dma_start(
                out=stg32[sl][:], in_=vo_d[c * 128:(c + 1) * 128, :]), f"DV{sl}")
            S.wait("scalar", tl)
            if c >= 2:
                S.wait("scalar", tok_vo[c - 2][1])
            th = S.run("scalar", lambda e, c=c, sl=sl: e.activation(
                out=stgh[sl][:], in_=stg32[sl][:], func=AF.Identity), sem="A")
            S.wait("vector", th)
            tlo = S.run("vector", lambda e, c=c, sl=sl: e.scalar_tensor_tensor(
                out=stgl[sl][:], in0=stg32[sl][:], scalar=1.0, in1=stgh[sl][:],
                op0=ALU.mult, op1=ALU.subtract), sem="V")
            S.wait("gpsimd", th)
            d1 = S.dma("gpsimd", lambda e, c=c, sl=sl: e.dma_start(
                out=voh_s[c], in_=stgh[sl][:]), "DG")
            S.wait("gpsimd", tlo)
            d2 = S.dma("gpsimd", lambda e, c=c, sl=sl: e.dma_start(
                out=vol_s[c], in_=stgl[sl][:]), "DG")
            tok_vo[c] = (tlo, d1, d2)

        tok_wc = {}
        for oc in range(NCH):
            if oc >= 1:
                S.wait("sync", tok_wc[oc - 1])
            tl = S.dma("sync", lambda e, oc=oc: e.dma_start(
                out=wint[:], in_=wt_d[oc * 128:(oc + 1) * 128, :]), "DS")
            S.wait("vector", tl)
            tok_wc[oc] = S.run("vector", lambda e, oc=oc: e.tensor_scalar(
                out=wm1[:, oc * D:(oc + 1) * D], in0=wint[:], scalar1=-1.0,
                scalar2=None, op0=ALU.add), sem="V")
        prev_ev = None
        prev_dma = None
        S.wait("tensor", tok_wc[NCH - 1])
        for ic in range(NCH):
            if prev_ev is not None:
                S.wait("tensor", prev_ev)
            ttr = None
            for oc in range(NCH):
                ttr = S.run("tensor", lambda e, ic=ic, oc=oc: e.matmul(
                    ps_w[:, oc * 128:(oc + 1) * 128],
                    wm1[:, oc * D + ic * 128: oc * D + (ic + 1) * 128],
                    idb_sb[:], is_transpose=True, start=(oc == 0),
                    stop=(oc == NCH - 1)), sem="T" if oc == NCH - 1 else None)
            S.wait("scalar", ttr)
            if prev_dma is not None:
                S.wait("scalar", prev_dma)
            prev_ev = S.run("scalar", lambda e: e.activation(
                out=wstg[:], in_=ps_w[:], func=AF.Identity), sem="A")
            S.wait("gpsimd", prev_ev)
            prev_dma = S.dma("gpsimd", lambda e, ic=ic: e.dma_start(
                out=wtb_s[ic], in_=wstg[:]), "DG")

    DMAS = ("DS", "DG", "DQ", "DX", "DV0", "DV1", "DV2", "DW0", "DW1", "DW2", "DW3")
    prev_phase_toks = [S.now(x) for x in (("T", "A", "V") + DMAS)]

    for s in range(NSEQ):
        pa = contextlib.ExitStack()
        P = pa.enter_context
        qk_hi = P(nc.sbuf_tensor(f"a{s}_qk_hi", [128, NCH * D], BF16))
        qk_lo = P(nc.sbuf_tensor(f"a{s}_qk_lo", [128, NCH * D], BF16))
        qkstg = [P(nc.sbuf_tensor(f"a{s}_qkstg{i}", [128, D], F32)) for i in range(2)]
        xt = [P(nc.sbuf_tensor(f"a{s}_xt{i}", [128, D], F32)) for i in range(2)]
        xn32 = [P(nc.sbuf_tensor(f"a{s}_xn32_{i}", [128, D], F32)) for i in range(2)]
        sqsc = P(nc.sbuf_tensor(f"a{s}_sqsc", [128, D], F32))
        msA = P(nc.sbuf_tensor(f"a{s}_msA", [128, NT], F32))
        nmA = P(nc.sbuf_tensor(f"a{s}_nmA", [128, NT], F32))
        ssA = P(nc.sbuf_tensor(f"a{s}_ssA", [128, NT], F32))
        vcA = P(nc.sbuf_tensor(f"a{s}_vcA", [128, NT], F32))
        msqA = P(nc.sbuf_tensor(f"a{s}_msqA", [128, NT], F32))
        rsA = P(nc.sbuf_tensor(f"a{s}_rsA", [128, NT], F32))
        nmrA = P(nc.sbuf_tensor(f"a{s}_nmrA", [128, NT], F32))
        qsth = P(nc.sbuf_tensor(f"a{s}_qsth", [128, NCH * 256], BF16))
        qstl = P(nc.sbuf_tensor(f"a{s}_qstl", [128, NCH * 256], BF16))
        ps_trb = [P(nc.psum_tensor(f"a{s}_pstrb{i}", [128, 1024], BF16)) for i in range(2)]
        ps_q = [P(nc.psum_tensor(f"a{s}_psq{i}", [128, 512], F32)) for i in range(2)]

        with pa:
            S.barrier(prev_phase_toks)

            tok_qk = {}
            for c in range(NCH):
                sl = c % 2
                if c >= 2:
                    S.wait("sync", tok_qk[c - 2])
                tl = S.dma("sync", lambda e, c=c, sl=sl: e.dma_start(
                    out=qkstg[sl][:], in_=qk_d[c * 128:(c + 1) * 128, :]), f"DW{sl}")
                S.wait("scalar", tl)
                th = S.run("scalar", lambda e, c=c, sl=sl: e.activation(
                    out=qk_hi[:, c * D:(c + 1) * D], in_=qkstg[sl][:],
                    func=AF.Identity), sem="A")
                S.wait("vector", th)
                tok_qk[c] = S.run("vector", lambda e, c=c, sl=sl: e.scalar_tensor_tensor(
                    out=qk_lo[:, c * D:(c + 1) * D], in0=qkstg[sl][:], scalar=1.0,
                    in1=qk_hi[:, c * D:(c + 1) * D], op0=ALU.mult,
                    op1=ALU.subtract), sem="V")
            tok_qk_done = tok_qk[NCH - 1]

            fmh = xn_fm_hi[:].rearrange("p (c t) -> p c t", c=NCH)
            fml = xn_fm_lo[:].rearrange("p (c t) -> p c t", c=NCH)
            qh_v = qsth[:].rearrange("p (c t) -> p c t", c=NCH)
            ql_v = qstl[:].rearrange("p (c t) -> p c t", c=NCH)

            tok_xn_hi = {}
            tok_xn_lo = {}
            tok_fm_ev = {}
            tok_qev = {}
            tok_qdma = None

            tok_sq = {}
            for k in range(NT):
                kb = k % 2
                if k >= 2:
                    S.wait("sync", tok_sq[k - 2])
                txl = S.dma("sync", lambda e, k=k, kb=kb, s=s: e.dma_start(
                    out=xt[kb][:], in_=x_d[s, k * 128:(k + 1) * 128, :]), f"DV{kb}")
                S.wait("vector", txl)
                S.run("vector", lambda e, k=k, kb=kb: e.reduce_sum(
                    out=msA[:, k:k + 1], in_=xt[kb][:], axis=AX.X))
                tok_sq[k] = S.run("vector", lambda e, k=k, kb=kb:
                                  e.scalar_tensor_tensor(
                    out=sqsc[:], in0=xt[kb][:], scalar=1.0, in1=xt[kb][:],
                    op0=ALU.mult, op1=ALU.mult,
                    accum_out=ssA[:, k:k + 1]), sem="V")

            tA = tok_sq[NT - 1]
            S.wait("scalar", tA)
            tnmA = S.run("scalar", lambda e: e.activation(
                out=nmA[:], in_=msA[:], func=AF.Identity, scale=-1.0 / D), sem="A")
            S.wait("vector", tnmA)
            S.run("vector", lambda e: e.tensor_tensor(
                out=msqA[:], in0=nmA[:], in1=nmA[:], op=ALU.mult))
            S.drain("vector")
            S.run("vector", lambda e: e.tensor_scalar(
                out=vcA[:], in0=ssA[:], scalar1=1.0 / D, scalar2=None, op0=ALU.mult))
            S.drain("vector")
            tvc = S.run("vector", lambda e: e.tensor_tensor(
                out=vcA[:], in0=vcA[:], in1=msqA[:], op=ALU.subtract), sem="V")
            S.wait("scalar", tvc)
            S.run("scalar", lambda e: e.activation(
                out=msqA[:], in_=vcA[:], func=AF.Ln, scale=1.0, bias=eps_c[:]), sem="A")
            S.drain("scalar")
            trsA = S.run("scalar", lambda e: e.activation(
                out=rsA[:], in_=msqA[:], func=AF.Exp, scale=-0.5), sem="A")
            S.wait("vector", trsA)
            tnrA = S.run("vector", lambda e: e.tensor_tensor(
                out=nmrA[:], in0=nmA[:], in1=rsA[:], op=ALU.mult), sem="V")
            S.wait("scalar", tnrA)

            def emit_q(p):
                nonlocal tok_qdma
                S.wait("tensor", tok_qk_done)
                for t in (2 * p, 2 * p + 1):
                    for pl in range(2):
                        S.wait("tensor", tok_fm_ev[(t, pl)])
                for et in range(NCH):
                    eb = et % 2
                    prev = (tok_qev.get((p, et - 2)) if et >= 2
                            else tok_qev.get((p - 1, et + 6)))
                    if prev is not None:
                        S.wait("tensor", prev[1])
                    tq = None
                    passes = ((qk_hi, xn_fm_hi), (qk_lo, xn_fm_hi),
                              (qk_hi, xn_fm_lo))
                    for c in range(NCH):
                        for pi, (lh, rh) in enumerate(passes):
                            last = (c == NCH - 1 and pi == 2)
                            tq = S.run("tensor", lambda e, eb=eb, c=c, lh=lh,
                                       rh=rh, et=et, p=p, first=(c == 0 and pi == 0),
                                       last=last: e.matmul(
                                           ps_q[eb][:, 0:256],
                                           lh[:, c * D + et * 128:
                                              c * D + (et + 1) * 128],
                                           rh[:, c * T + p * 256:
                                              c * T + (p + 1) * 256],
                                           start=first, stop=last),
                                       sem="T" if last else None)
                    S.wait("scalar", tq)
                    if tok_qdma is not None and et == 0:
                        S.wait("scalar", tok_qdma)
                    teh = S.run("scalar", lambda e, eb=eb, et=et: e.activation(
                        out=qsth[:, et * 256:(et + 1) * 256],
                        in_=ps_q[eb][:, 0:256], func=AF.Identity), sem="A")
                    S.wait("vector", teh)
                    tel = S.run("vector", lambda e, eb=eb, et=et:
                                e.scalar_tensor_tensor(
                                    out=qstl[:, et * 256:(et + 1) * 256],
                                    in0=ps_q[eb][:, 0:256], scalar=1.0,
                                    in1=qsth[:, et * 256:(et + 1) * 256],
                                    op0=ALU.mult, op1=ALU.subtract), sem="V")
                    tok_qev[(p, et)] = (teh, tel)
                S.wait("gpsimd", tok_qev[(p, NCH - 1)][1])
                S.dma("gpsimd", lambda e, p=p, s=s: e.dma_start(
                    out=qs_hi[s, :, :, p * 256:(p + 1) * 256], in_=qh_v), "DG")
                tok_qdma = S.dma("gpsimd", lambda e, p=p, s=s: e.dma_start(
                    out=qs_lo[s, :, :, p * 256:(p + 1) * 256], in_=ql_v), "DG")

            for k in range(NT):
                kb = k % 2
                kx = k % 2
                if k >= 2:
                    S.wait("sync", tok_xn_hi[k - 2])
                else:
                    S.wait("sync", tok_sq[NT - 2 + k])
                txl = S.dma("sync", lambda e, k=k, kb=kb, s=s: e.dma_start(
                    out=xt[kb][:], in_=x_d[s, k * 128:(k + 1) * 128, :]), f"DV{kb}")
                S.wait("scalar", txl)
                if k >= 2:
                    S.wait("scalar", tok_fm_ev[(k - 2, 0)])
                    S.wait("scalar", tok_fm_ev[(k - 2, 1)])
                    S.wait("scalar", tok_xn_lo[k - 2])
                txn = S.run("scalar", lambda e, kb=kb, kx=kx, k=k: e.activation(
                    out=xn32[kx][:], in_=xt[kb][:], func=AF.Identity,
                    scale=rsA[:, k:k + 1], bias=nmrA[:, k:k + 1]), sem="A")
                tok_xn_hi[k] = S.run("scalar", lambda e, k=k, kx=kx: e.activation(
                    out=xn_tm_hi[:, k * D:(k + 1) * D], in_=xn32[kx][:],
                    func=AF.Identity), sem="A")
                S.wait("vector", tok_xn_hi[k])
                tok_xn_lo[k] = S.run("vector", lambda e, k=k, kx=kx: e.scalar_tensor_tensor(
                    out=xn_tm_lo[:, k * D:(k + 1) * D], in0=xn32[kx][:], scalar=1.0,
                    in1=xn_tm_hi[:, k * D:(k + 1) * D], op0=ALU.mult,
                    op1=ALU.subtract), sem="V")
                for pl in range(2):
                    tm = xn_tm_hi if pl == 0 else xn_tm_lo
                    fmv = fmh if pl == 0 else fml
                    prev = tok_fm_ev.get((k - 1, pl))
                    if prev is not None:
                        S.wait("tensor", prev)
                    if pl == 0:
                        S.wait("tensor", tok_xn_hi[k])
                    S.wait("tensor", tok_xn_lo[k])
                    ttr = None
                    for c in range(NCH):
                        ttr = S.run("tensor", lambda e, k=k, c=c, pl=pl, tm=tm:
                                    e.matmul(
                            ps_trb[pl][:, c * 128:(c + 1) * 128],
                            tm[:, k * D + c * 128:k * D + (c + 1) * 128],
                            idb_sb[:], is_transpose=True, start=(c == 0),
                            stop=(c == NCH - 1)), sem="T" if c == NCH - 1 else None)
                    dst = fmv[:, :, k * 128:(k + 1) * 128]
                    S.wait("scalar", ttr)
                    tok_fm_ev[(k, pl)] = S.run("scalar", lambda e, pl=pl, dst=dst:
                                               e.activation(
                        out=dst, in_=ps_trb[pl][:], func=AF.Identity), sem="A")
                if k % 2 == 1 and k >= 3:
                    emit_q((k - 3) // 2)
            emit_q(NT // 2 - 1)

            phase_a_toks = [S.now(x) for x in (("T", "A", "V") + DMAS)]

        mn = contextlib.ExitStack()
        P = mn.enter_context
        probs = P(nc.sbuf_tensor(f"m{s}_probs", [128, T], F32))
        q_hi = P(nc.sbuf_tensor(f"m{s}_q_hi", [128, NCH * 128], BF16))
        q_lo = P(nc.sbuf_tensor(f"m{s}_q_lo", [128, NCH * 128], BF16))
        at_hi = P(nc.sbuf_tensor(f"m{s}_at_hi", [128, T], BF16))
        at_lo = P(nc.sbuf_tensor(f"m{s}_at_lo", [128, T], BF16))
        av32 = P(nc.sbuf_tensor(f"m{s}_av32", [128, D], F32))
        avt_hi = P(nc.sbuf_tensor(f"m{s}_avt_hi", [128, D], BF16))
        avt_lo = P(nc.sbuf_tensor(f"m{s}_avt_lo", [128, D], BF16))
        vo_hi = [P(nc.sbuf_tensor(f"m{s}_vo_hi{i}", [128, D], BF16)) for i in range(3)]
        vo_lo = [P(nc.sbuf_tensor(f"m{s}_vo_lo{i}", [128, D], BF16)) for i in range(3)]
        wtc = [P(nc.sbuf_tensor(f"m{s}_wtc{i}", [128, D], BF16)) for i in range(4)]
        x_t = P(nc.sbuf_tensor(f"m{s}_x_t", [128, D], F32))
        x2 = [P(nc.sbuf_tensor(f"m{s}_x2_{i}", [128, D], F32)) for i in range(2)]
        rq = av32
        xq32 = [P(nc.sbuf_tensor(f"m{s}_xq32_{i}", [128, D], F32)) for i in range(2)]
        xintT = P(nc.sbuf_tensor(f"m{s}_xintT", [128, D], BF16))
        ostg = P(nc.sbuf_tensor(f"m{s}_ostg", [128, D], F32))
        ps_sc = P(nc.psum_tensor(f"m{s}_ps_sc", [128, T], F32))
        ps_av = P(nc.psum_tensor(f"m{s}_ps_av", [128, D], F32))
        ps_at = [P(nc.psum_tensor(f"m{s}_ps_at{i}", [128, 512], F32))
                 for i in range(2)]

        with mn:
            S.barrier(phase_a_toks)

            qv_hi = q_hi[:].rearrange("p (c t) -> p c t", c=NCH)
            qv_lo = q_lo[:].rearrange("p (c t) -> p c t", c=NCH)

            tok_exp = {}
            tok_at_ev = {}
            tok_at_g = {}
            tok_sc = {}
            tok_av_h = {}
            tok_av_ev = {}
            tok_avt_g = {}
            tok_avt_ev = {}
            tok_out_c = {}
            tok_out_ev = {}
            tok_xt = {}
            tok_xt_ev = {}
            tok_ff_c = {}
            tok_ff_ev = {}
            tok_xint = {}
            tok_rz = {}
            tok_store = {}
            tok_vold = {}
            tok_wtd = {}
            tok_qld = {}
            tok_xld = {}
            tok_tz = {}

            for k in range(NT + 2):
                j1, j2 = k - 1, k - 2
                ci = k % 4

                if 0 <= j1 < NT:
                    for c in (0, 1, 2):
                        S.wait("sync", tok_out_c.get((j1 - 1, c + 5)))
                        S.dma("sync", lambda e, c=c: e.dma_start(
                            out=vo_hi[c % 3][:], in_=voh_s[c]), f"DV{c % 3}")
                        tok_vold[(j1, c)] = S.dma("sync", lambda e, c=c: e.dma_start(
                            out=vo_lo[c % 3][:], in_=vol_s[c]), f"DV{c % 3}")
                if 0 <= j2 < NT:
                    for c in (0, 1, 2, 3):
                        S.wait("sync", tok_ff_c.get((j2 - 1, c + 4)))
                        tok_wtd[(j2, c)] = S.dma("sync", lambda e, c=c: e.dma_start(
                            out=wtc[c % 4][:], in_=wtb_s[c]), f"DW{c % 4}")

                if k < NT:
                    if k == 0:
                        S.wait("sync", phase_a_toks[4])
                        S.dma("sync", lambda e, s=s: e.dma_start(
                            out=qv_hi, in_=qs_hi[s, :, :, 0:128]), "DQ")
                        tok_qld[0] = S.dma("sync", lambda e, s=s: e.dma_start(
                            out=qv_lo, in_=qs_lo[s, :, :, 0:128]), "DQ")
                    S.wait("tensor", tok_qld[k])
                    S.wait("tensor", tok_exp.get(k - 1))
                    if k - 1 >= 0:
                        S.wait("tensor", tok_at_ev[(k - 1, 2)][0])
                        S.wait("tensor", tok_at_ev[(k - 1, 2)][1])
                        S.wait("tensor", tok_at_ev[(k - 1, 3)][0])
                        S.wait("tensor", tok_at_ev[(k - 1, 3)][1])
                    S.wait("tensor", tok_xt_ev.get(k - 3))
                    S.wait("tensor", tok_ff_ev.get(k - 3))
                    t = None
                    passes = ((q_hi, xn_fm_hi), (q_lo, xn_fm_hi), (q_hi, xn_fm_lo))
                    for g in range(4):
                        for c in range(NCH):
                            for pi, (lh, rh) in enumerate(passes):
                                last = (g == 3 and c == NCH - 1 and pi == 2)
                                t = S.run("tensor", lambda e, g=g, c=c, lh=lh, rh=rh,
                                          first=(c == 0 and pi == 0),
                                          stop=(c == NCH - 1 and pi == 2): e.matmul(
                                              ps_sc[:, g * 512:(g + 1) * 512],
                                              lh[:, c * 128:(c + 1) * 128],
                                              rh[:, c * T + g * 512:
                                                 c * T + (g + 1) * 512],
                                              start=first, stop=stop),
                                          sem="T" if last else None)
                    tok_sc[k] = t
                    if k + 1 < NT:
                        S.wait("sync", tok_sc[k])
                        S.dma("sync", lambda e, k=k, s=s: e.dma_start(
                            out=qv_hi, in_=qs_hi[s, :, :, (k + 1) * 128:(k + 2) * 128]),
                            "DQ")
                        tok_qld[k + 1] = S.dma("sync", lambda e, k=k, s=s: e.dma_start(
                            out=qv_lo, in_=qs_lo[s, :, :, (k + 1) * 128:(k + 2) * 128]),
                            "DQ")

                    S.wait("vector", tok_sc[k])
                    S.run("vector", lambda e, ci=ci: e.reduce_max(
                        out=nmx_c[:, ci:ci + 1], in_=ps_sc[:], axis=AX.X,
                        negate=True), sem="V")
                    S.drain("vector")
                    tmx2 = S.run("vector", lambda e, ci=ci: e.tensor_scalar(
                        out=nmx2_c[:, ci:ci + 1], in0=nmx_c[:, ci:ci + 1],
                        scalar1=1.0 / 32.0, scalar2=None, op0=ALU.mult), sem="V")
                    S.wait("scalar", tmx2)
                    if k >= 1:
                        S.wait("scalar", tok_at_g[(k - 1, 3)])
                    S.wait("scalar", tok_xint.get(k - 2))
                    tok_exp[k] = S.run("scalar", lambda e, ci=ci: e.activation(
                        out=probs[:], in_=ps_sc[:], func=AF.Exp,
                        bias=nmx2_c[:, ci:ci + 1], scale=1.0 / 32.0,
                        accum_out=zs_c[:, ci:ci + 1]), sem="A")
                    S.wait("vector", tok_exp[k])
                    tok_rz[k] = S.run("vector", lambda e, ci=ci: e.reciprocal(
                        out=rz_c[:, ci:ci + 1], in_=zs_c[:, ci:ci + 1]), sem="V")

                if 0 <= j1 < NT:
                    for g in range(4):
                        S.wait("tensor", tok_at_ev[(j1, g)][0])
                        S.wait("tensor", tok_at_ev[(j1, g)][1])
                    S.wait("tensor", tok_out_ev.get(j1 - 1))
                    passes = ((at_hi, xn_tm_hi), (at_lo, xn_tm_hi), (at_hi, xn_tm_lo))
                    for h in range(2):
                        t = None
                        for j in range(NT):
                            for pi, (lh, rh) in enumerate(passes):
                                last = (j == NT - 1 and pi == 2)
                                t = S.run("tensor", lambda e, h=h, j=j, lh=lh, rh=rh,
                                          first=(j == 0 and pi == 0), last=last:
                                          e.matmul(
                                              ps_av[:, h * 512:(h + 1) * 512],
                                              lh[:, j * 128:(j + 1) * 128],
                                              rh[:, j * D + h * 512:
                                                 j * D + (h + 1) * 512],
                                              start=first, stop=last),
                                          sem="T" if last else None)
                        tok_av_h[(j1, h)] = t
                        S.wait("scalar", t)
                        S.wait("scalar", tok_xint.get(j1 - 1))
                        tok_av_ev[(j1, h)] = S.run("scalar", lambda e, h=h:
                                                   e.activation(
                            out=av32[:, h * 512:(h + 1) * 512],
                            in_=ps_av[:, h * 512:(h + 1) * 512],
                            func=AF.Identity), sem="A")

                if 0 <= j1 < NT:
                    for g in range(2):
                        S.wait("tensor", tok_av_ev[(j1, g)])
                        S.wait("tensor", tok_at_ev[(k - 1, g)][0])
                        S.wait("tensor", tok_at_ev[(k - 1, g)][1])
                        t = None
                        for j in range(4):
                            c = g * 4 + j
                            t = S.run("tensor", lambda e, g=g, j=j, c=c: e.matmul(
                                ps_at[g][:, j * 128:(j + 1) * 128],
                                av32[:, c * 128:(c + 1) * 128], idf_sb[:],
                                is_transpose=True, start=(j == 0), stop=(j == 3)),
                                sem="T" if j == 3 else None)
                        tok_avt_g[(j1, g)] = t
                        S.wait("scalar", t)
                        teh = S.run("scalar", lambda e, g=g: e.activation(
                            out=avt_hi[:, g * 512:(g + 1) * 512], in_=ps_at[g][:],
                            func=AF.Identity), sem="A")
                        S.wait("vector", teh)
                        tel = S.run("vector", lambda e, g=g: e.scalar_tensor_tensor(
                            out=avt_lo[:, g * 512:(g + 1) * 512], in0=ps_at[g][:],
                            scalar=1.0, in1=avt_hi[:, g * 512:(g + 1) * 512],
                            op0=ALU.mult, op1=ALU.subtract), sem="V")
                        tok_avt_ev[(j1, g)] = (teh, tel)

                if 0 <= j1 < NT:
                    for c in range(NCH):
                        gi = 0 if c < 4 else 1
                        S.wait("tensor", tok_avt_ev[(j1, gi)][0])
                        S.wait("tensor", tok_avt_ev[(j1, gi)][1])
                        S.wait("tensor", tok_vold[(j1, c)])
                        passes = ((avt_hi, vo_hi[c % 3]), (avt_lo, vo_hi[c % 3]),
                                  (avt_hi, vo_lo[c % 3]))
                        t = None
                        for h in range(2):
                            for pi, (lh, rh) in enumerate(passes):
                                t = S.run("tensor", lambda e, h=h, c=c, lh=lh, rh=rh,
                                          first=(c == 0 and pi == 0),
                                          stop=(c == NCH - 1 and pi == 2): e.matmul(
                                              ps_av[:, h * 512:(h + 1) * 512],
                                              lh[:, c * 128:(c + 1) * 128],
                                              rh[:, h * 512:(h + 1) * 512],
                                              start=first, stop=stop),
                                          sem="T" if (h == 1 and pi == 2) else None)
                        tok_out_c[(j1, c)] = t
                        if c + 3 < NCH:
                            S.wait("sync", t)
                            S.dma("sync", lambda e, c=c: e.dma_start(
                                out=vo_hi[c % 3][:], in_=voh_s[c + 3]), f"DV{c % 3}")
                            tok_vold[(j1, c + 3)] = S.dma("sync", lambda e, c=c:
                                                          e.dma_start(
                                out=vo_lo[c % 3][:], in_=vol_s[c + 3]), f"DV{c % 3}")

                    cj = j1 % 4
                    S.wait("vector", tok_out_c[(j1, NCH - 1)])
                    S.wait("vector", tok_xld[j1])
                    S.wait("vector", tok_rz[j1])
                    tok_out_ev[j1] = S.run("vector", lambda e, j1=j1, cj=cj:
                                           e.scalar_tensor_tensor(
                        out=x2[j1 % 2][:], in0=ps_av[:],
                        scalar=rz_c[:, cj:cj + 1], in1=x_t[:],
                        op0=ALU.mult, op1=ALU.add), sem="V")

                    if j1 + 1 < NT:
                        S.wait("sync", tok_out_ev[j1])
                        tok_xld[j1 + 1] = S.dma("sync", lambda e, j1=j1, s=s: e.dma_start(
                            out=x_t[:], in_=x_d[s, (j1 + 1) * 128:(j1 + 2) * 128, :]),
                            "DX")
                if k == 0:
                    tok_xld[0] = S.dma("sync", lambda e, s=s: e.dma_start(
                        out=x_t[:], in_=x_d[s, 0:128, :]), "DX")

                if k < NT:
                    S.wait("tensor", tok_exp[k])
                    for g in range(4):
                        if g < 2 and 0 <= j1 < NT:
                            S.wait("tensor", tok_avt_ev[(j1, g)][0])
                            S.wait("tensor", tok_avt_ev[(j1, g)][1])
                        pdst = (ps_at[g] if g < 2 else None)
                        t = None
                        for j in range(4):
                            c = g * 4 + j
                            dst = (ps_at[g][:, j * 128:(j + 1) * 128] if g < 2 else
                                   ps_sc[:, (g - 2) * 512 + j * 128:
                                         (g - 2) * 512 + (j + 1) * 128])
                            t = S.run("tensor", lambda e, dst=dst, c=c: e.matmul(
                                dst, probs[:, c * 128:(c + 1) * 128], idf_sb[:],
                                is_transpose=True, start=(c % 4 == 0),
                                stop=(c % 4 == 3)), sem="T" if j == 3 else None)
                        tok_at_g[(k, g)] = t
                        S.wait("scalar", t)
                        psrc = (ps_at[g][:] if g < 2 else
                                ps_sc[:, (g - 2) * 512:(g - 1) * 512])
                        teh = S.run("scalar", lambda e, g=g, psrc=psrc: e.activation(
                            out=at_hi[:, g * 512:(g + 1) * 512], in_=psrc,
                            func=AF.Identity), sem="A")
                        S.wait("vector", teh)
                        tel = S.run("vector", lambda e, g=g, psrc=psrc:
                                    e.scalar_tensor_tensor(
                            out=at_lo[:, g * 512:(g + 1) * 512], in0=psrc,
                            scalar=1.0, in1=at_hi[:, g * 512:(g + 1) * 512],
                            op0=ALU.mult, op1=ALU.subtract), sem="V")
                        tok_at_ev[(k, g)] = (teh, tel)

                if 0 <= j2 < NT:
                    kk = min(k, NT - 1)
                    S.wait("tensor", tok_xint[j2])
                    tev = []
                    for g in range(2):
                        S.wait("tensor", tok_at_ev[(kk, 2 + g)][0])
                        S.wait("tensor", tok_at_ev[(kk, 2 + g)][1])
                        t = None
                        for j in range(4):
                            c = g * 4 + j
                            t = S.run("tensor", lambda e, g=g, j=j, c=c, j2=j2:
                                      e.matmul(
                                          ps_sc[:, g * 512 + j * 128:
                                                g * 512 + (j + 1) * 128],
                                          xq32[j2 % 2][:, c * 128:(c + 1) * 128],
                                          idf_sb[:], is_transpose=True,
                                          start=(j == 0), stop=(j == 3)),
                                      sem="T" if j == 3 else None)
                        S.wait("scalar", t)
                        tev.append(S.run("scalar", lambda e, g=g: e.activation(
                            out=xintT[:, g * 512:(g + 1) * 512],
                            in_=ps_sc[:, g * 512:(g + 1) * 512],
                            func=AF.Identity), sem="A"))
                    tok_xt_ev[j2] = tev[-1]

                if 0 <= j2 < NT:
                    S.wait("tensor", tok_xt_ev[j2])
                    S.wait("tensor", tok_bds)
                    for h in range(2):
                        S.run("tensor", lambda e, h=h: e.matmul(
                            ps_sc[:, 1024 + h * 512:1024 + (h + 1) * 512],
                            ones_sb[0:1, :], bds_hi[0:1, h * 512:(h + 1) * 512],
                            start=True, stop=False))
                        S.run("tensor", lambda e, h=h: e.matmul(
                            ps_sc[:, 1024 + h * 512:1024 + (h + 1) * 512],
                            ones_sb[0:1, :], bds_lo[0:1, h * 512:(h + 1) * 512],
                            start=False, stop=False))
                    for c in range(NCH):
                        S.wait("tensor", tok_wtd[(j2, c)])
                        t = None
                        for h in range(2):
                            t = S.run("tensor", lambda e, h=h, c=c, j2=j2: e.matmul(
                                ps_sc[:, 1024 + h * 512:1024 + (h + 1) * 512],
                                xintT[:, c * 128:(c + 1) * 128],
                                wtc[c % 4][:, h * 512:(h + 1) * 512],
                                start=False, stop=(c == NCH - 1)),
                                sem="T" if h == 1 else None)
                        tok_ff_c[(j2, c)] = t
                        if c + 4 < NCH:
                            S.wait("sync", t)
                            tok_wtd[(j2, c + 4)] = S.dma("sync", lambda e, c=c:
                                                         e.dma_start(
                                out=wtc[c % 4][:], in_=wtb_s[c + 4]), f"DW{c % 4}")

                if 0 <= j1 < NT:
                    cj = j1 % 4
                    S.wait("vector", tok_at_g.get((min(k, NT - 1), 3)))
                    S.run("vector", lambda e, j1=j1, cj=cj: e.reduce_sum(
                        out=msum_c[:, cj:cj + 1], in_=x2[j1 % 2][:], axis=AX.X))
                    S.drain("vector")
                    tnm = S.run("vector", lambda e, cj=cj: e.tensor_scalar(
                        out=negmd_c[:, cj:cj + 1], in0=msum_c[:, cj:cj + 1],
                        scalar1=-1.0 / D, scalar2=None, op0=ALU.mult), sem="V")
                    S.drain("vector")
                    S.wait("vector", tok_xint.get(j1 - 1))
                    S.run("vector", lambda e, j1=j1, cj=cj: e.tensor_scalar(
                        out=probs[:, 0:1024], in0=x2[j1 % 2][:],
                        scalar1=negmd_c[:, cj:cj + 1], scalar2=None, op0=ALU.add))
                    tsq = S.run("vector", lambda e, cj=cj: e.scalar_tensor_tensor(
                        out=rq[:], in0=probs[:, 0:1024], scalar=1.0, in1=probs[:, 0:1024],
                        op0=ALU.mult, op1=ALU.mult,
                        accum_out=ssq_c[:, cj:cj + 1]), sem="V")
                    S.wait("scalar", tsq)
                    S.run("scalar", lambda e, cj=cj: e.activation(
                        out=std_c[:, cj:cj + 1], in_=ssq_c[:, cj:cj + 1],
                        func=AF.Ln, scale=1.0 / D, bias=eps_c[:]), sem="A")
                    S.drain("scalar")
                    trst = S.run("scalar", lambda e, cj=cj: e.activation(
                        out=rstd_c[:, cj:cj + 1], in_=std_c[:, cj:cj + 1],
                        func=AF.Exp, scale=-0.5), sem="A")
                    S.wait("vector", trst)
                    S.wait("vector", tok_rscol)
                    S.drain("vector")
                    S.run("vector", lambda e, cj=cj: e.tensor_scalar(
                        out=rsx_c[:, cj:cj + 1], in0=rstd_c[:, cj:cj + 1],
                        scalar1=rs_col[:], scalar2=None, op0=ALU.mult))
                    S.drain("vector")
                    tb2 = S.run("vector", lambda e, cj=cj: e.tensor_scalar(
                        out=b2_c[:, cj:cj + 1], in0=negmd_c[:, cj:cj + 1],
                        scalar1=rsx_c[:, cj:cj + 1], scalar2=128.5, op0=ALU.mult,
                        op1=ALU.add), sem="V")
                    S.wait("scalar", tb2)
                    tz = S.run("scalar", lambda e, j1=j1, cj=cj: e.activation(
                        out=probs[:, 0:1024], in_=x2[j1 % 2][:], func=AF.Identity,
                        scale=rsx_c[:, cj:cj + 1], bias=b2_c[:, cj:cj + 1]), sem="A")
                    tok_tz[j1] = tz


                if 0 <= j2 < NT:
                    S.wait("vector", tok_ff_c[(j2, NCH - 1)])
                    S.wait("vector", tok_scol)
                    S.wait("vector", tok_store.get(j2 - 1))
                    tok_ff_ev[j2] = S.run("vector", lambda e, j2=j2:
                                          e.scalar_tensor_tensor(
                        out=ostg[:], in0=ps_sc[:, 1024:2048], scalar=s_col[:],
                        in1=x2[j2 % 2][:], op0=ALU.mult, op1=ALU.add), sem="V")
                    S.wait("gpsimd", tok_ff_ev[j2])
                    tok_store[j2] = S.dma("gpsimd", lambda e, j2=j2, s=s: e.dma_start(
                        out=out_d[s, j2 * 128:(j2 + 1) * 128, :], in_=ostg[:]), "DG")

                if 0 <= j1 < NT:
                    S.wait("vector", tok_tz[j1])
                    S.wait("vector", tok_store.get(j1 - 1))
                    S.run("vector", lambda e: e.tensor_scalar(
                        out=probs[:, 0:1024], in0=probs[:, 0:1024], scalar1=0.9, scalar2=255.9,
                        op0=ALU.max, op1=ALU.min), sem="V")
                    S.run("vector", lambda e: e.tensor_copy(out=rq[:], in_=probs[:, 0:1024]),
                          sem="V")
                    for b in (128.0, 64.0, 32.0, 16.0, 8.0, 4.0, 2.0, 1.0):
                        S.run("vector", lambda e, b=b: e.tensor_scalar(
                            out=ostg[:], in0=rq[:], scalar1=b, scalar2=b,
                            op0=ALU.is_ge, op1=ALU.mult))
                        S.run("vector", lambda e: e.tensor_tensor(
                            out=rq[:], in0=rq[:], in1=ostg[:], op=ALU.subtract))
                    S.run("vector", lambda e: e.tensor_tensor(
                        out=ostg[:], in0=probs[:, 0:1024], in1=rq[:], op=ALU.subtract), sem="V")
                    S.wait("vector", tok_xt_ev.get(j1 - 2))
                    tok_xint[j1] = S.run("vector", lambda e, j1=j1: e.tensor_scalar(
                        out=xq32[j1 % 2][:], in0=ostg[:], scalar1=-128.0,
                        scalar2=None, op0=ALU.add), sem="V")


            prev_phase_toks = [S.now(x) for x in (("T", "A", "V") + DMAS)]

    S.wait("gpsimd", S.now("DG"))

    with es:
        with nc.Block() as block:
            S.realize(block, sems)
    return nc


_NC_CACHE = []


def kernel(L0_attn_x, L0_attn_QK, L0_attn_VO, L0_ffn_ternary_weight,
           L0_ffn_bias, L0_ffn_act_scale):
    x = np.ascontiguousarray(np.asarray(L0_attn_x, dtype=np.float32))
    qk = np.ascontiguousarray(np.asarray(L0_attn_QK, dtype=np.float32))
    vo = np.ascontiguousarray(np.asarray(L0_attn_VO, dtype=np.float32))
    wt = np.ascontiguousarray(np.asarray(L0_ffn_ternary_weight, dtype=np.int32))
    fb = np.ascontiguousarray(np.asarray(L0_ffn_bias, dtype=np.float32).reshape(1, D))
    asc = np.ascontiguousarray(np.asarray(L0_ffn_act_scale,
                                          dtype=np.float32).reshape(1, 1))
    idf = np.eye(128, dtype=np.float32)
    idb = np.eye(128, dtype=BF)
    onesb = np.ones((1, 128), dtype=BF)

    shards = x.reshape(8, NSEQ, T, D)
    in_maps = [{"x": shards[c], "qk": qk, "vo": vo, "wt": wt, "fbias": fb,
                "ascale": asc, "idf": idf, "idb": idb, "onesb": onesb}
               for c in range(8)]
    if not _NC_CACHE:
        _NC_CACHE.append(build_nc())
    nc = _NC_CACHE[0]
    res = run_bass_kernel_spmd(nc, in_maps, core_ids=list(range(8)))
    out = np.stack([np.asarray(r["out"]) for r in res.results], axis=0)
    return out.reshape(16, T, D)


if __name__ == "__main__":
    nc = build_nc()
    print("built", sum(1 for _ in nc.all_instructions()), "instructions")


# version 12 (final)
# speedup vs baseline: 1.0200x; 1.0029x over previous
import contextlib
import numpy as np
import ml_dtypes
import concourse.bass as bass
import concourse.mybir as mybir
from concourse.bass_utils import run_bass_kernel_spmd

F32 = mybir.dt.float32
BF16 = mybir.dt.bfloat16
I32 = mybir.dt.int32
AF = mybir.ActivationFunctionType
ALU = mybir.AluOpType
AX = mybir.AxisListType
BF = ml_dtypes.bfloat16

NSEQ = 2
NT = 16
NCH = 8
D = 1024
T = 2048

ENGS = ("tensor", "scalar", "vector", "sync", "gpsimd")


class Sched:

    def __init__(self):
        self.ops = {e: [] for e in ENGS}
        self.cnt = {}

    def run(self, eng, fn, sem=None, amt=1):
        self.ops[eng].append(("inst", fn, sem, amt))
        if sem is not None:
            self.cnt[sem] = self.cnt.get(sem, 0) + amt
            return (sem, self.cnt[sem])
        return None

    def dma(self, eng, fn, sem):
        return self.run(eng, fn, sem=sem, amt=16)

    def wait(self, eng, tok):
        if tok is None:
            return
        sem, val = tok
        if val <= 0:
            return
        assert self.cnt.get(sem, 0) >= val, f"wait on future token {sem}={val}"
        ops = self.ops[eng]
        if ops and ops[-1][0] == "wait" and ops[-1][1] == sem:
            ops[-1] = ("wait", sem, max(ops[-1][2], val))
        else:
            ops.append(("wait", sem, val))

    def drain(self, eng):
        self.ops[eng].append(("drain",))

    def barrier(self, toks):
        for e in ENGS:
            for tok in toks:
                self.wait(e, tok)

    def now(self, sem):
        return (sem, self.cnt.get(sem, 0))

    def realize(self, block, sems):
        def make(oplist):
            def body(e):
                for op in oplist:
                    if op[0] == "wait":
                        e.wait_ge(sems[op[1]], op[2])
                    elif op[0] == "drain":
                        e.drain()
                    else:
                        _, fn, sem, amt = op
                        inst = fn(e)
                        if sem is not None:
                            inst.then_inc(sems[sem], amt)
            return body

        for eng_name in ENGS:
            if self.ops[eng_name]:
                getattr(block, eng_name)(make(self.ops[eng_name]))


def build_nc():
    nc = bass.Bass("TRN2", target_bir_lowering=False)

    x_d = nc.dram_tensor("x", [NSEQ, T, D], F32, kind="ExternalInput").ap()
    qk_d = nc.dram_tensor("qk", [D, D], F32, kind="ExternalInput").ap()
    vo_d = nc.dram_tensor("vo", [D, D], F32, kind="ExternalInput").ap()
    wt_d = nc.dram_tensor("wt", [D, D], I32, kind="ExternalInput").ap()
    bias_d = nc.dram_tensor("fbias", [1, D], F32, kind="ExternalInput").ap()
    asc_d = nc.dram_tensor("ascale", [1, 1], F32, kind="ExternalInput").ap()
    idf_d = nc.dram_tensor("idf", [128, 128], F32, kind="ExternalInput").ap()
    idb_d = nc.dram_tensor("idb", [128, 128], BF16, kind="ExternalInput").ap()
    ones_d = nc.dram_tensor("onesb", [1, 128], BF16, kind="ExternalInput").ap()
    out_d = nc.dram_tensor("out", [NSEQ, T, D], F32, kind="ExternalOutput").ap()

    qs_hi = nc.dram_tensor("qs_hi", [NSEQ, 128, NCH, T], BF16, kind="Internal").ap()
    qs_lo = nc.dram_tensor("qs_lo", [NSEQ, 128, NCH, T], BF16, kind="Internal").ap()
    voh_s = nc.dram_tensor("voh_s", [NCH, 128, D], BF16, kind="Internal").ap()
    vol_s = nc.dram_tensor("vol_s", [NCH, 128, D], BF16, kind="Internal").ap()
    wtb_s = nc.dram_tensor("wtb_s", [NCH, 128, D], BF16, kind="Internal").ap()

    S = Sched()
    es = contextlib.ExitStack()
    E = es.enter_context

    xn_tm_hi = E(nc.sbuf_tensor("xn_tm_hi", [128, NT * D], BF16))
    xn_tm_lo = E(nc.sbuf_tensor("xn_tm_lo", [128, NT * D], BF16))
    xn_fm_hi = E(nc.sbuf_tensor("xn_fm_hi", [128, NCH * T], BF16))
    xn_fm_lo = E(nc.sbuf_tensor("xn_fm_lo", [128, NCH * T], BF16))
    idf_sb = E(nc.sbuf_tensor("idf_sb", [128, 128], F32))
    idb_sb = E(nc.sbuf_tensor("idb_sb", [128, 128], BF16))
    ones_sb = E(nc.sbuf_tensor("ones_sb", [1, 128], BF16))
    bds_hi = E(nc.sbuf_tensor("bds_hi", [1, D], BF16))
    bds_lo = E(nc.sbuf_tensor("bds_lo", [1, D], BF16))
    s_col = E(nc.sbuf_tensor("s_col", [128, 1], F32))
    rs_col = E(nc.sbuf_tensor("rs_col", [128, 1], F32))
    eps_c = E(nc.sbuf_tensor("eps_c", [128, 1], F32))
    msum_c = E(nc.sbuf_tensor("msum_c", [128, 4], F32))
    negmd_c = E(nc.sbuf_tensor("negmd_c", [128, 4], F32))
    ssq_c = E(nc.sbuf_tensor("ssq_c", [128, 4], F32))
    std_c = E(nc.sbuf_tensor("std_c", [128, 4], F32))
    rstd_c = E(nc.sbuf_tensor("rstd_c", [128, 4], F32))
    nmr_c = E(nc.sbuf_tensor("nmr_c", [128, 4], F32))
    rsx_c = E(nc.sbuf_tensor("rsx_c", [128, 4], F32))
    b2_c = E(nc.sbuf_tensor("b2_c", [128, 4], F32))
    nmx_c = E(nc.sbuf_tensor("nmx_c", [128, 4], F32))
    nmx2_c = E(nc.sbuf_tensor("nmx2_c", [128, 4], F32))
    zs_c = E(nc.sbuf_tensor("zs_c", [128, 4], F32))
    rz_c = E(nc.sbuf_tensor("rz_c", [128, 4], F32))

    sems = {}
    for name in ("T", "A", "V", "DS", "DG", "DQ", "DX",
                 "DV0", "DV1", "DV2", "DW0", "DW1", "DW2", "DW3"):
        sems[name] = E(nc.semaphore("sem_" + name))

    pro = contextlib.ExitStack()
    P = pro.enter_context
    stg32 = [P(nc.sbuf_tensor(f"p_stg32_{i}", [128, D], F32)) for i in range(2)]
    stgh = [P(nc.sbuf_tensor(f"p_stgh_{i}", [128, D], BF16)) for i in range(2)]
    stgl = [P(nc.sbuf_tensor(f"p_stgl_{i}", [128, D], BF16)) for i in range(2)]
    wint = P(nc.sbuf_tensor("p_wint", [128, D], I32))
    bias_sb = P(nc.sbuf_tensor("bias_sb", [1, D], F32))
    bdsr = P(nc.sbuf_tensor("bdsr", [1, D], F32))
    asc_sb = P(nc.sbuf_tensor("asc_sb", [1, 1], F32))
    s_sb = P(nc.sbuf_tensor("s_sb", [1, 1], F32))
    srecip = P(nc.sbuf_tensor("srecip", [1, 1], F32))
    s_row = P(nc.sbuf_tensor("s_row", [1, 128], F32))
    rs_row = P(nc.sbuf_tensor("rs_row", [1, 128], F32))
    wm1 = P(nc.sbuf_tensor("p_wm1", [128, NCH * D], BF16))
    wstg = P(nc.sbuf_tensor("p_wstg", [128, D], BF16))
    ps_w = P(nc.psum_tensor("p_ps_w", [128, 1024], BF16))
    ps_bc = P(nc.psum_tensor("p_ps_bc", [128, 1], F32))

    with pro:
        S.dma("sync", lambda e: e.dma_start(out=idf_sb[:], in_=idf_d), "DS")
        S.dma("sync", lambda e: e.dma_start(out=idb_sb[:], in_=idb_d), "DS")
        S.dma("sync", lambda e: e.dma_start(out=ones_sb[:], in_=ones_d), "DS")
        S.dma("sync", lambda e: e.dma_start(out=bias_sb[:], in_=bias_d), "DS")
        t_consts = S.dma("sync", lambda e: e.dma_start(out=asc_sb[:], in_=asc_d), "DS")

        S.wait("vector", t_consts)
        S.run("vector", lambda e: e.memset(eps_c[:], 1e-5))
        S.run("vector", lambda e: e.tensor_scalar(
            out=s_sb[:], in0=asc_sb[:], scalar1=1e-5, scalar2=None, op0=ALU.max))
        S.drain("vector")
        S.run("vector", lambda e: e.reciprocal(out=srecip[:], in_=s_sb[:]))
        S.drain("vector")
        t = S.run("vector", lambda e: e.tensor_scalar(
            out=bdsr[:], in0=bias_sb[:], scalar1=srecip[:], scalar2=None,
            op0=ALU.mult), sem="V")
        S.wait("scalar", t)
        th = S.run("scalar", lambda e: e.activation(
            out=bds_hi[:], in_=bdsr[:], func=AF.Identity), sem="A")
        S.wait("vector", th)
        tok_bds = S.run("vector", lambda e: e.scalar_tensor_tensor(
            out=bds_lo[:], in0=bdsr[:], scalar=1.0, in1=bds_hi[:],
            op0=ALU.mult, op1=ALU.subtract), sem="V")
        t = S.run("vector", lambda e: e.tensor_scalar(
            out=s_row[:], in0=ones_sb[:], scalar1=s_sb[:], scalar2=None,
            op0=ALU.mult), sem="V")
        t2 = S.run("vector", lambda e: e.tensor_scalar(
            out=rs_row[:], in0=ones_sb[:], scalar1=srecip[:], scalar2=None,
            op0=ALU.mult), sem="V")
        S.wait("tensor", t2)
        tt = S.run("tensor", lambda e: e.matmul(
            ps_bc[:], s_row[0:1, :], idf_sb[0:1, 0:1], is_transpose=True,
            start=True, stop=True), sem="T")
        S.wait("scalar", tt)
        tok_scol = S.run("scalar", lambda e: e.activation(
            out=s_col[:], in_=ps_bc[:], func=AF.Identity), sem="A")
        S.wait("tensor", tok_scol)
        tt = S.run("tensor", lambda e: e.matmul(
            ps_bc[:], rs_row[0:1, :], idf_sb[0:1, 0:1], is_transpose=True,
            start=True, stop=True), sem="T")
        S.wait("scalar", tt)
        tok_rscol = S.run("scalar", lambda e: e.activation(
            out=rs_col[:], in_=ps_bc[:], func=AF.Identity), sem="A")

        S.wait("sync", tok_bds)
        tok_vo = {}
        for c in range(NCH):
            sl = c % 2
            if c >= 2:
                S.wait("sync", tok_vo[c - 2][0])
            tl = S.dma("sync", lambda e, c=c, sl=sl: e.dma_start(
                out=stg32[sl][:], in_=vo_d[c * 128:(c + 1) * 128, :]), f"DV{sl}")
            S.wait("scalar", tl)
            if c >= 2:
                S.wait("scalar", tok_vo[c - 2][1])
            th = S.run("scalar", lambda e, c=c, sl=sl: e.activation(
                out=stgh[sl][:], in_=stg32[sl][:], func=AF.Identity), sem="A")
            S.wait("vector", th)
            tlo = S.run("vector", lambda e, c=c, sl=sl: e.scalar_tensor_tensor(
                out=stgl[sl][:], in0=stg32[sl][:], scalar=1.0, in1=stgh[sl][:],
                op0=ALU.mult, op1=ALU.subtract), sem="V")
            S.wait("gpsimd", th)
            d1 = S.dma("gpsimd", lambda e, c=c, sl=sl: e.dma_start(
                out=voh_s[c], in_=stgh[sl][:]), "DG")
            S.wait("gpsimd", tlo)
            d2 = S.dma("gpsimd", lambda e, c=c, sl=sl: e.dma_start(
                out=vol_s[c], in_=stgl[sl][:]), "DG")
            tok_vo[c] = (tlo, d1, d2)

        tok_wc = {}
        for oc in range(NCH):
            if oc >= 1:
                S.wait("sync", tok_wc[oc - 1])
            tl = S.dma("sync", lambda e, oc=oc: e.dma_start(
                out=wint[:], in_=wt_d[oc * 128:(oc + 1) * 128, :]), "DS")
            S.wait("vector", tl)
            tok_wc[oc] = S.run("vector", lambda e, oc=oc: e.tensor_scalar(
                out=wm1[:, oc * D:(oc + 1) * D], in0=wint[:], scalar1=-1.0,
                scalar2=None, op0=ALU.add), sem="V")
        prev_ev = None
        prev_dma = None
        S.wait("tensor", tok_wc[NCH - 1])
        for ic in range(NCH):
            if prev_ev is not None:
                S.wait("tensor", prev_ev)
            ttr = None
            for oc in range(NCH):
                ttr = S.run("tensor", lambda e, ic=ic, oc=oc: e.matmul(
                    ps_w[:, oc * 128:(oc + 1) * 128],
                    wm1[:, oc * D + ic * 128: oc * D + (ic + 1) * 128],
                    idb_sb[:], is_transpose=True, start=(oc == 0),
                    stop=(oc == NCH - 1)), sem="T" if oc == NCH - 1 else None)
            S.wait("scalar", ttr)
            if prev_dma is not None:
                S.wait("scalar", prev_dma)
            prev_ev = S.run("scalar", lambda e: e.activation(
                out=wstg[:], in_=ps_w[:], func=AF.Identity), sem="A")
            S.wait("gpsimd", prev_ev)
            prev_dma = S.dma("gpsimd", lambda e, ic=ic: e.dma_start(
                out=wtb_s[ic], in_=wstg[:]), "DG")

    DMAS = ("DS", "DG", "DQ", "DX", "DV0", "DV1", "DV2", "DW0", "DW1", "DW2", "DW3")
    prev_phase_toks = [S.now(x) for x in (("T", "A", "V") + DMAS)]

    for s in range(NSEQ):
        pa = contextlib.ExitStack()
        P = pa.enter_context
        qk_hi = P(nc.sbuf_tensor(f"a{s}_qk_hi", [128, NCH * D], BF16))
        qk_lo = P(nc.sbuf_tensor(f"a{s}_qk_lo", [128, NCH * D], BF16))
        qkstg = [P(nc.sbuf_tensor(f"a{s}_qkstg{i}", [128, D], F32)) for i in range(2)]
        xt = [P(nc.sbuf_tensor(f"a{s}_xt{i}", [128, D], F32)) for i in range(2)]
        xn32 = [P(nc.sbuf_tensor(f"a{s}_xn32_{i}", [128, D], F32)) for i in range(2)]
        sqsc = P(nc.sbuf_tensor(f"a{s}_sqsc", [128, D], F32))
        msA = P(nc.sbuf_tensor(f"a{s}_msA", [128, NT], F32))
        nmA = P(nc.sbuf_tensor(f"a{s}_nmA", [128, NT], F32))
        ssA = P(nc.sbuf_tensor(f"a{s}_ssA", [128, NT], F32))
        vcA = P(nc.sbuf_tensor(f"a{s}_vcA", [128, NT], F32))
        msqA = P(nc.sbuf_tensor(f"a{s}_msqA", [128, NT], F32))
        rsA = P(nc.sbuf_tensor(f"a{s}_rsA", [128, NT], F32))
        nmrA = P(nc.sbuf_tensor(f"a{s}_nmrA", [128, NT], F32))
        qsth = P(nc.sbuf_tensor(f"a{s}_qsth", [128, NCH * 256], BF16))
        qstl = P(nc.sbuf_tensor(f"a{s}_qstl", [128, NCH * 256], BF16))
        ps_trb = [P(nc.psum_tensor(f"a{s}_pstrb{i}", [128, 1024], BF16)) for i in range(2)]
        ps_q = [P(nc.psum_tensor(f"a{s}_psq{i}", [128, 512], F32)) for i in range(2)]

        with pa:
            S.barrier(prev_phase_toks)

            tok_qk = {}
            for c in range(NCH):
                sl = c % 2
                if c >= 2:
                    S.wait("sync", tok_qk[c - 2])
                tl = S.dma("sync", lambda e, c=c, sl=sl: e.dma_start(
                    out=qkstg[sl][:], in_=qk_d[c * 128:(c + 1) * 128, :]), f"DW{sl}")
                S.wait("scalar", tl)
                th = S.run("scalar", lambda e, c=c, sl=sl: e.activation(
                    out=qk_hi[:, c * D:(c + 1) * D], in_=qkstg[sl][:],
                    func=AF.Identity), sem="A")
                S.wait("vector", th)
                tok_qk[c] = S.run("vector", lambda e, c=c, sl=sl: e.scalar_tensor_tensor(
                    out=qk_lo[:, c * D:(c + 1) * D], in0=qkstg[sl][:], scalar=1.0,
                    in1=qk_hi[:, c * D:(c + 1) * D], op0=ALU.mult,
                    op1=ALU.subtract), sem="V")
            tok_qk_done = tok_qk[NCH - 1]

            fmh = xn_fm_hi[:].rearrange("p (c t) -> p c t", c=NCH)
            fml = xn_fm_lo[:].rearrange("p (c t) -> p c t", c=NCH)
            qh_v = qsth[:].rearrange("p (c t) -> p c t", c=NCH)
            ql_v = qstl[:].rearrange("p (c t) -> p c t", c=NCH)

            tok_xn_hi = {}
            tok_xn_lo = {}
            tok_fm_ev = {}
            tok_qev = {}
            tok_qdma = None

            tok_sq = {}
            for k in range(NT):
                kb = k % 2
                if k >= 2:
                    S.wait("sync", tok_sq[k - 2])
                txl = S.dma("sync", lambda e, k=k, kb=kb, s=s: e.dma_start(
                    out=xt[kb][:], in_=x_d[s, k * 128:(k + 1) * 128, :]), f"DV{kb}")
                S.wait("vector", txl)
                S.run("vector", lambda e, k=k, kb=kb: e.reduce_sum(
                    out=msA[:, k:k + 1], in_=xt[kb][:], axis=AX.X))
                tok_sq[k] = S.run("vector", lambda e, k=k, kb=kb:
                                  e.scalar_tensor_tensor(
                    out=sqsc[:], in0=xt[kb][:], scalar=1.0, in1=xt[kb][:],
                    op0=ALU.mult, op1=ALU.mult,
                    accum_out=ssA[:, k:k + 1]), sem="V")

            tA = tok_sq[NT - 1]
            S.wait("scalar", tA)
            tnmA = S.run("scalar", lambda e: e.activation(
                out=nmA[:], in_=msA[:], func=AF.Identity, scale=-1.0 / D), sem="A")
            S.wait("vector", tnmA)
            S.run("vector", lambda e: e.tensor_tensor(
                out=msqA[:], in0=nmA[:], in1=nmA[:], op=ALU.mult))
            S.drain("vector")
            S.run("vector", lambda e: e.tensor_scalar(
                out=vcA[:], in0=ssA[:], scalar1=1.0 / D, scalar2=None, op0=ALU.mult))
            S.drain("vector")
            tvc = S.run("vector", lambda e: e.tensor_tensor(
                out=vcA[:], in0=vcA[:], in1=msqA[:], op=ALU.subtract), sem="V")
            S.wait("scalar", tvc)
            S.run("scalar", lambda e: e.activation(
                out=msqA[:], in_=vcA[:], func=AF.Ln, scale=1.0, bias=eps_c[:]), sem="A")
            S.drain("scalar")
            trsA = S.run("scalar", lambda e: e.activation(
                out=rsA[:], in_=msqA[:], func=AF.Exp, scale=-0.5), sem="A")
            S.wait("vector", trsA)
            tnrA = S.run("vector", lambda e: e.tensor_tensor(
                out=nmrA[:], in0=nmA[:], in1=rsA[:], op=ALU.mult), sem="V")
            S.wait("scalar", tnrA)

            def emit_q(p):
                nonlocal tok_qdma
                S.wait("tensor", tok_qk_done)
                for t in (2 * p, 2 * p + 1):
                    for pl in range(2):
                        S.wait("tensor", tok_fm_ev[(t, pl)])
                for et in range(NCH):
                    eb = et % 2
                    prev = (tok_qev.get((p, et - 2)) if et >= 2
                            else tok_qev.get((p - 1, et + 6)))
                    if prev is not None:
                        S.wait("tensor", prev[1])
                    tq = None
                    passes = ((qk_hi, xn_fm_hi), (qk_lo, xn_fm_hi),
                              (qk_hi, xn_fm_lo))
                    for c in range(NCH):
                        for pi, (lh, rh) in enumerate(passes):
                            last = (c == NCH - 1 and pi == 2)
                            tq = S.run("tensor", lambda e, eb=eb, c=c, lh=lh,
                                       rh=rh, et=et, p=p, first=(c == 0 and pi == 0),
                                       last=last: e.matmul(
                                           ps_q[eb][:, 0:256],
                                           lh[:, c * D + et * 128:
                                              c * D + (et + 1) * 128],
                                           rh[:, c * T + p * 256:
                                              c * T + (p + 1) * 256],
                                           start=first, stop=last),
                                       sem="T" if last else None)
                    S.wait("scalar", tq)
                    if tok_qdma is not None and et == 0:
                        S.wait("scalar", tok_qdma)
                    teh = S.run("scalar", lambda e, eb=eb, et=et: e.activation(
                        out=qsth[:, et * 256:(et + 1) * 256],
                        in_=ps_q[eb][:, 0:256], func=AF.Identity), sem="A")
                    S.wait("vector", teh)
                    tel = S.run("vector", lambda e, eb=eb, et=et:
                                e.scalar_tensor_tensor(
                                    out=qstl[:, et * 256:(et + 1) * 256],
                                    in0=ps_q[eb][:, 0:256], scalar=1.0,
                                    in1=qsth[:, et * 256:(et + 1) * 256],
                                    op0=ALU.mult, op1=ALU.subtract), sem="V")
                    tok_qev[(p, et)] = (teh, tel)
                S.wait("gpsimd", tok_qev[(p, NCH - 1)][1])
                S.dma("gpsimd", lambda e, p=p, s=s: e.dma_start(
                    out=qs_hi[s, :, :, p * 256:(p + 1) * 256], in_=qh_v), "DG")
                tok_qdma = S.dma("gpsimd", lambda e, p=p, s=s: e.dma_start(
                    out=qs_lo[s, :, :, p * 256:(p + 1) * 256], in_=ql_v), "DG")

            for k in range(NT):
                kb = k % 2
                kx = k % 2
                if k >= 2:
                    S.wait("sync", tok_xn_hi[k - 2])
                else:
                    S.wait("sync", tok_sq[NT - 2 + k])
                txl = S.dma("sync", lambda e, k=k, kb=kb, s=s: e.dma_start(
                    out=xt[kb][:], in_=x_d[s, k * 128:(k + 1) * 128, :]), f"DV{kb}")
                S.wait("scalar", txl)
                if k >= 2:
                    S.wait("scalar", tok_fm_ev[(k - 2, 0)])
                    S.wait("scalar", tok_fm_ev[(k - 2, 1)])
                    S.wait("scalar", tok_xn_lo[k - 2])
                txn = S.run("scalar", lambda e, kb=kb, kx=kx, k=k: e.activation(
                    out=xn32[kx][:], in_=xt[kb][:], func=AF.Identity,
                    scale=rsA[:, k:k + 1], bias=nmrA[:, k:k + 1]), sem="A")
                tok_xn_hi[k] = S.run("scalar", lambda e, k=k, kx=kx: e.activation(
                    out=xn_tm_hi[:, k * D:(k + 1) * D], in_=xn32[kx][:],
                    func=AF.Identity), sem="A")
                S.wait("vector", tok_xn_hi[k])
                tok_xn_lo[k] = S.run("vector", lambda e, k=k, kx=kx: e.scalar_tensor_tensor(
                    out=xn_tm_lo[:, k * D:(k + 1) * D], in0=xn32[kx][:], scalar=1.0,
                    in1=xn_tm_hi[:, k * D:(k + 1) * D], op0=ALU.mult,
                    op1=ALU.subtract), sem="V")
                for pl in range(2):
                    tm = xn_tm_hi if pl == 0 else xn_tm_lo
                    fmv = fmh if pl == 0 else fml
                    prev = tok_fm_ev.get((k - 1, pl))
                    if prev is not None:
                        S.wait("tensor", prev)
                    if pl == 0:
                        S.wait("tensor", tok_xn_hi[k])
                    S.wait("tensor", tok_xn_lo[k])
                    ttr = None
                    for c in range(NCH):
                        ttr = S.run("tensor", lambda e, k=k, c=c, pl=pl, tm=tm:
                                    e.matmul(
                            ps_trb[pl][:, c * 128:(c + 1) * 128],
                            tm[:, k * D + c * 128:k * D + (c + 1) * 128],
                            idb_sb[:], is_transpose=True, start=(c == 0),
                            stop=(c == NCH - 1)), sem="T" if c == NCH - 1 else None)
                    dst = fmv[:, :, k * 128:(k + 1) * 128]
                    S.wait("scalar", ttr)
                    tok_fm_ev[(k, pl)] = S.run("scalar", lambda e, pl=pl, dst=dst:
                                               e.activation(
                        out=dst, in_=ps_trb[pl][:], func=AF.Identity), sem="A")
                if k % 2 == 1 and k >= 3:
                    emit_q((k - 3) // 2)
            emit_q(NT // 2 - 1)

            phase_a_toks = [S.now(x) for x in (("T", "A", "V") + DMAS)]

        mn = contextlib.ExitStack()
        P = mn.enter_context
        probs = P(nc.sbuf_tensor(f"m{s}_probs", [128, T], F32))
        q_hi = P(nc.sbuf_tensor(f"m{s}_q_hi", [128, NCH * 128], BF16))
        q_lo = P(nc.sbuf_tensor(f"m{s}_q_lo", [128, NCH * 128], BF16))
        at_hi = P(nc.sbuf_tensor(f"m{s}_at_hi", [128, T], BF16))
        at_lo = P(nc.sbuf_tensor(f"m{s}_at_lo", [128, T], BF16))
        av32 = P(nc.sbuf_tensor(f"m{s}_av32", [128, D], F32))
        avt_hi = P(nc.sbuf_tensor(f"m{s}_avt_hi", [128, D], BF16))
        avt_lo = P(nc.sbuf_tensor(f"m{s}_avt_lo", [128, D], BF16))
        vo_hi = [P(nc.sbuf_tensor(f"m{s}_vo_hi{i}", [128, D], BF16)) for i in range(3)]
        vo_lo = [P(nc.sbuf_tensor(f"m{s}_vo_lo{i}", [128, D], BF16)) for i in range(3)]
        wtc = [P(nc.sbuf_tensor(f"m{s}_wtc{i}", [128, D], BF16)) for i in range(4)]
        x_t = P(nc.sbuf_tensor(f"m{s}_x_t", [128, D], F32))
        x2 = [P(nc.sbuf_tensor(f"m{s}_x2_{i}", [128, D], F32)) for i in range(2)]
        rq = av32
        xq32 = [P(nc.sbuf_tensor(f"m{s}_xq32_{i}", [128, D], F32)) for i in range(2)]
        xintT = P(nc.sbuf_tensor(f"m{s}_xintT", [128, D], BF16))
        ostg = P(nc.sbuf_tensor(f"m{s}_ostg", [128, D], F32))
        ps_sc = P(nc.psum_tensor(f"m{s}_ps_sc", [128, T], F32))
        ps_av = P(nc.psum_tensor(f"m{s}_ps_av", [128, D], F32))
        ps_at = [P(nc.psum_tensor(f"m{s}_ps_at{i}", [128, 512], F32))
                 for i in range(2)]

        with mn:
            S.barrier(phase_a_toks)

            qv_hi = q_hi[:].rearrange("p (c t) -> p c t", c=NCH)
            qv_lo = q_lo[:].rearrange("p (c t) -> p c t", c=NCH)

            tok_exp = {}
            tok_at_ev = {}
            tok_at_g = {}
            tok_sc = {}
            tok_av_h = {}
            tok_av_ev = {}
            tok_avt_g = {}
            tok_avt_ev = {}
            tok_out_c = {}
            tok_out_ev = {}
            tok_xt = {}
            tok_xt_ev = {}
            tok_ff_c = {}
            tok_ff_ev = {}
            tok_xint = {}
            tok_rz = {}
            tok_store = {}
            tok_vold = {}
            tok_wtd = {}
            tok_qld = {}
            tok_xld = {}
            tok_tz = {}

            for k in range(NT + 2):
                j1, j2 = k - 1, k - 2
                ci = k % 4

                if 0 <= j1 < NT:
                    for c in (0, 1, 2):
                        S.wait("sync", tok_out_c.get((j1 - 1, c + 5)))
                        S.dma("sync", lambda e, c=c: e.dma_start(
                            out=vo_hi[c % 3][:], in_=voh_s[c]), f"DV{c % 3}")
                        tok_vold[(j1, c)] = S.dma("sync", lambda e, c=c: e.dma_start(
                            out=vo_lo[c % 3][:], in_=vol_s[c]), f"DV{c % 3}")
                if 0 <= j2 < NT:
                    for c in (0, 1, 2, 3):
                        S.wait("sync", tok_ff_c.get((j2 - 1, c + 4)))
                        tok_wtd[(j2, c)] = S.dma("sync", lambda e, c=c: e.dma_start(
                            out=wtc[c % 4][:], in_=wtb_s[c]), f"DW{c % 4}")

                if k < NT:
                    if k == 0:
                        S.wait("sync", phase_a_toks[4])
                        S.dma("sync", lambda e, s=s: e.dma_start(
                            out=qv_hi, in_=qs_hi[s, :, :, 0:128]), "DQ")
                        tok_qld[0] = S.dma("sync", lambda e, s=s: e.dma_start(
                            out=qv_lo, in_=qs_lo[s, :, :, 0:128]), "DQ")
                    S.wait("tensor", tok_qld[k])
                    S.wait("tensor", tok_exp.get(k - 1))
                    if k - 1 >= 0:
                        S.wait("tensor", tok_at_ev[(k - 1, 2)][0])
                        S.wait("tensor", tok_at_ev[(k - 1, 2)][1])
                        S.wait("tensor", tok_at_ev[(k - 1, 3)][0])
                        S.wait("tensor", tok_at_ev[(k - 1, 3)][1])
                    S.wait("tensor", tok_xt_ev.get(k - 3))
                    S.wait("tensor", tok_ff_ev.get(k - 3))
                    t = None
                    passes = ((q_hi, xn_fm_hi), (q_lo, xn_fm_hi), (q_hi, xn_fm_lo))
                    for g in range(4):
                        for c in range(NCH):
                            for pi, (lh, rh) in enumerate(passes):
                                last = (g == 3 and c == NCH - 1 and pi == 2)
                                t = S.run("tensor", lambda e, g=g, c=c, lh=lh, rh=rh,
                                          first=(c == 0 and pi == 0),
                                          stop=(c == NCH - 1 and pi == 2): e.matmul(
                                              ps_sc[:, g * 512:(g + 1) * 512],
                                              lh[:, c * 128:(c + 1) * 128],
                                              rh[:, c * T + g * 512:
                                                 c * T + (g + 1) * 512],
                                              start=first, stop=stop),
                                          sem="T" if last else None)
                    tok_sc[k] = t
                    if k + 1 < NT:
                        S.wait("sync", tok_sc[k])
                        S.dma("sync", lambda e, k=k, s=s: e.dma_start(
                            out=qv_hi, in_=qs_hi[s, :, :, (k + 1) * 128:(k + 2) * 128]),
                            "DQ")
                        tok_qld[k + 1] = S.dma("sync", lambda e, k=k, s=s: e.dma_start(
                            out=qv_lo, in_=qs_lo[s, :, :, (k + 1) * 128:(k + 2) * 128]),
                            "DQ")

                    S.wait("vector", tok_sc[k])
                    S.run("vector", lambda e, ci=ci: e.reduce_max(
                        out=nmx_c[:, ci:ci + 1], in_=ps_sc[:], axis=AX.X,
                        negate=True), sem="V")
                    S.drain("vector")
                    tmx2 = S.run("vector", lambda e, ci=ci: e.tensor_scalar(
                        out=nmx2_c[:, ci:ci + 1], in0=nmx_c[:, ci:ci + 1],
                        scalar1=1.0 / 32.0, scalar2=None, op0=ALU.mult), sem="V")
                    S.wait("scalar", tmx2)
                    if k >= 1:
                        S.wait("scalar", tok_at_g[(k - 1, 3)])
                    S.wait("scalar", tok_xint.get(k - 2))
                    tok_exp[k] = S.run("scalar", lambda e, ci=ci: e.activation(
                        out=probs[:], in_=ps_sc[:], func=AF.Exp,
                        bias=nmx2_c[:, ci:ci + 1], scale=1.0 / 32.0,
                        accum_out=zs_c[:, ci:ci + 1]), sem="A")
                    S.wait("vector", tok_exp[k])
                    tok_rz[k] = S.run("vector", lambda e, ci=ci: e.reciprocal(
                        out=rz_c[:, ci:ci + 1], in_=zs_c[:, ci:ci + 1]), sem="V")

                if 0 <= j1 < NT:
                    for g in range(4):
                        S.wait("tensor", tok_at_ev[(j1, g)][0])
                        S.wait("tensor", tok_at_ev[(j1, g)][1])
                    S.wait("tensor", tok_out_ev.get(j1 - 1))
                    passes = ((at_hi, xn_tm_hi), (at_lo, xn_tm_hi), (at_hi, xn_tm_lo))
                    for h in range(2):
                        t = None
                        for j in range(NT):
                            for pi, (lh, rh) in enumerate(passes):
                                last = (j == NT - 1 and pi == 2)
                                t = S.run("tensor", lambda e, h=h, j=j, lh=lh, rh=rh,
                                          first=(j == 0 and pi == 0), last=last:
                                          e.matmul(
                                              ps_av[:, h * 512:(h + 1) * 512],
                                              lh[:, j * 128:(j + 1) * 128],
                                              rh[:, j * D + h * 512:
                                                 j * D + (h + 1) * 512],
                                              start=first, stop=last),
                                          sem="T" if last else None)
                        tok_av_h[(j1, h)] = t
                        S.wait("scalar", t)
                        S.wait("scalar", tok_xint.get(j1 - 1))
                        tok_av_ev[(j1, h)] = S.run("scalar", lambda e, h=h:
                                                   e.activation(
                            out=av32[:, h * 512:(h + 1) * 512],
                            in_=ps_av[:, h * 512:(h + 1) * 512],
                            func=AF.Identity), sem="A")

                if 0 <= j1 < NT:
                    for g in range(2):
                        S.wait("tensor", tok_av_ev[(j1, g)])
                        S.wait("tensor", tok_at_ev[(k - 1, g)][0])
                        S.wait("tensor", tok_at_ev[(k - 1, g)][1])
                        t = None
                        for j in range(4):
                            c = g * 4 + j
                            t = S.run("tensor", lambda e, g=g, j=j, c=c: e.matmul(
                                ps_at[g][:, j * 128:(j + 1) * 128],
                                av32[:, c * 128:(c + 1) * 128], idf_sb[:],
                                is_transpose=True, start=(j == 0), stop=(j == 3)),
                                sem="T" if j == 3 else None)
                        tok_avt_g[(j1, g)] = t
                        S.wait("scalar", t)
                        teh = S.run("scalar", lambda e, g=g: e.activation(
                            out=avt_hi[:, g * 512:(g + 1) * 512], in_=ps_at[g][:],
                            func=AF.Identity), sem="A")
                        S.wait("vector", teh)
                        tel = S.run("vector", lambda e, g=g: e.scalar_tensor_tensor(
                            out=avt_lo[:, g * 512:(g + 1) * 512], in0=ps_at[g][:],
                            scalar=1.0, in1=avt_hi[:, g * 512:(g + 1) * 512],
                            op0=ALU.mult, op1=ALU.subtract), sem="V")
                        tok_avt_ev[(j1, g)] = (teh, tel)

                if 0 <= j1 < NT:
                    for c in range(NCH):
                        gi = 0 if c < 4 else 1
                        S.wait("tensor", tok_avt_ev[(j1, gi)][0])
                        S.wait("tensor", tok_avt_ev[(j1, gi)][1])
                        S.wait("tensor", tok_vold[(j1, c)])
                        passes = ((avt_hi, vo_hi[c % 3]), (avt_lo, vo_hi[c % 3]),
                                  (avt_hi, vo_lo[c % 3]))
                        t = None
                        for h in range(2):
                            for pi, (lh, rh) in enumerate(passes):
                                t = S.run("tensor", lambda e, h=h, c=c, lh=lh, rh=rh,
                                          first=(c == 0 and pi == 0),
                                          stop=(c == NCH - 1 and pi == 2): e.matmul(
                                              ps_av[:, h * 512:(h + 1) * 512],
                                              lh[:, c * 128:(c + 1) * 128],
                                              rh[:, h * 512:(h + 1) * 512],
                                              start=first, stop=stop),
                                          sem="T" if (h == 1 and pi == 2) else None)
                        tok_out_c[(j1, c)] = t
                        if c + 3 < NCH:
                            S.wait("sync", t)
                            S.dma("sync", lambda e, c=c: e.dma_start(
                                out=vo_hi[c % 3][:], in_=voh_s[c + 3]), f"DV{c % 3}")
                            tok_vold[(j1, c + 3)] = S.dma("sync", lambda e, c=c:
                                                          e.dma_start(
                                out=vo_lo[c % 3][:], in_=vol_s[c + 3]), f"DV{c % 3}")

                    cj = j1 % 4
                    S.wait("vector", tok_out_c[(j1, NCH - 1)])
                    S.wait("vector", tok_xld[j1])
                    S.wait("vector", tok_rz[j1])
                    tok_out_ev[j1] = S.run("vector", lambda e, j1=j1, cj=cj:
                                           e.scalar_tensor_tensor(
                        out=x2[j1 % 2][:], in0=ps_av[:],
                        scalar=rz_c[:, cj:cj + 1], in1=x_t[:],
                        op0=ALU.mult, op1=ALU.add), sem="V")

                    if j1 + 1 < NT:
                        S.wait("sync", tok_out_ev[j1])
                        tok_xld[j1 + 1] = S.dma("sync", lambda e, j1=j1, s=s: e.dma_start(
                            out=x_t[:], in_=x_d[s, (j1 + 1) * 128:(j1 + 2) * 128, :]),
                            "DX")
                if k == 0:
                    tok_xld[0] = S.dma("sync", lambda e, s=s: e.dma_start(
                        out=x_t[:], in_=x_d[s, 0:128, :]), "DX")

                if k < NT:
                    S.wait("tensor", tok_exp[k])
                    for g in range(4):
                        if g < 2 and 0 <= j1 < NT:
                            S.wait("tensor", tok_avt_ev[(j1, g)][0])
                            S.wait("tensor", tok_avt_ev[(j1, g)][1])
                        pdst = (ps_at[g] if g < 2 else None)
                        t = None
                        for j in range(4):
                            c = g * 4 + j
                            dst = (ps_at[g][:, j * 128:(j + 1) * 128] if g < 2 else
                                   ps_sc[:, (g - 2) * 512 + j * 128:
                                         (g - 2) * 512 + (j + 1) * 128])
                            t = S.run("tensor", lambda e, dst=dst, c=c: e.matmul(
                                dst, probs[:, c * 128:(c + 1) * 128], idf_sb[:],
                                is_transpose=True, start=(c % 4 == 0),
                                stop=(c % 4 == 3)), sem="T" if j == 3 else None)
                        tok_at_g[(k, g)] = t
                        S.wait("scalar", t)
                        psrc = (ps_at[g][:] if g < 2 else
                                ps_sc[:, (g - 2) * 512:(g - 1) * 512])
                        teh = S.run("scalar", lambda e, g=g, psrc=psrc: e.activation(
                            out=at_hi[:, g * 512:(g + 1) * 512], in_=psrc,
                            func=AF.Identity), sem="A")
                        S.wait("vector", teh)
                        tel = S.run("vector", lambda e, g=g, psrc=psrc:
                                    e.scalar_tensor_tensor(
                            out=at_lo[:, g * 512:(g + 1) * 512], in0=psrc,
                            scalar=1.0, in1=at_hi[:, g * 512:(g + 1) * 512],
                            op0=ALU.mult, op1=ALU.subtract), sem="V")
                        tok_at_ev[(k, g)] = (teh, tel)

                if 0 <= j2 < NT:
                    kk = min(k, NT - 1)
                    S.wait("tensor", tok_exp[kk])
                    S.wait("tensor", tok_ff_ev.get(j2 - 1))
                    S.wait("tensor", tok_bds)
                    for h in range(2):
                        S.run("tensor", lambda e, h=h: e.matmul(
                            ps_sc[:, 1024 + h * 512:1024 + (h + 1) * 512],
                            ones_sb[0:1, :], bds_hi[0:1, h * 512:(h + 1) * 512],
                            start=True, stop=False))
                        S.run("tensor", lambda e, h=h: e.matmul(
                            ps_sc[:, 1024 + h * 512:1024 + (h + 1) * 512],
                            ones_sb[0:1, :], bds_lo[0:1, h * 512:(h + 1) * 512],
                            start=False, stop=False))
                    S.wait("tensor", tok_xint[j2])
                    tev = []
                    for g in range(2):
                        S.wait("tensor", tok_at_ev[(kk, 2 + g)][0])
                        S.wait("tensor", tok_at_ev[(kk, 2 + g)][1])
                        t = None
                        for j in range(4):
                            c = g * 4 + j
                            t = S.run("tensor", lambda e, g=g, j=j, c=c, j2=j2:
                                      e.matmul(
                                          ps_sc[:, g * 512 + j * 128:
                                                g * 512 + (j + 1) * 128],
                                          xq32[j2 % 2][:, c * 128:(c + 1) * 128],
                                          idf_sb[:], is_transpose=True,
                                          start=(j == 0), stop=(j == 3)),
                                      sem="T" if j == 3 else None)
                        S.wait("scalar", t)
                        tev.append(S.run("scalar", lambda e, g=g: e.activation(
                            out=xintT[:, g * 512:(g + 1) * 512],
                            in_=ps_sc[:, g * 512:(g + 1) * 512],
                            func=AF.Identity), sem="A"))
                    tok_xt_ev[j2] = tev[-1]

                if 0 <= j2 < NT:
                    S.wait("tensor", tok_xt_ev[j2])
                    for c in range(NCH):
                        S.wait("tensor", tok_wtd[(j2, c)])
                        t = None
                        for h in range(2):
                            t = S.run("tensor", lambda e, h=h, c=c, j2=j2: e.matmul(
                                ps_sc[:, 1024 + h * 512:1024 + (h + 1) * 512],
                                xintT[:, c * 128:(c + 1) * 128],
                                wtc[c % 4][:, h * 512:(h + 1) * 512],
                                start=False, stop=(c == NCH - 1)),
                                sem="T" if h == 1 else None)
                        tok_ff_c[(j2, c)] = t
                        if c + 4 < NCH:
                            S.wait("sync", t)
                            tok_wtd[(j2, c + 4)] = S.dma("sync", lambda e, c=c:
                                                         e.dma_start(
                                out=wtc[c % 4][:], in_=wtb_s[c + 4]), f"DW{c % 4}")

                if 0 <= j1 < NT:
                    cj = j1 % 4
                    S.wait("vector", tok_at_g.get((min(k, NT - 1), 3)))
                    S.run("vector", lambda e, j1=j1, cj=cj: e.reduce_sum(
                        out=msum_c[:, cj:cj + 1], in_=x2[j1 % 2][:], axis=AX.X))
                    S.drain("vector")
                    tnm = S.run("vector", lambda e, cj=cj: e.tensor_scalar(
                        out=negmd_c[:, cj:cj + 1], in0=msum_c[:, cj:cj + 1],
                        scalar1=-1.0 / D, scalar2=None, op0=ALU.mult), sem="V")
                    S.drain("vector")
                    S.wait("vector", tok_xint.get(j1 - 1))
                    S.run("vector", lambda e, j1=j1, cj=cj: e.tensor_scalar(
                        out=probs[:, 0:1024], in0=x2[j1 % 2][:],
                        scalar1=negmd_c[:, cj:cj + 1], scalar2=None, op0=ALU.add))
                    tsq = S.run("vector", lambda e, cj=cj: e.scalar_tensor_tensor(
                        out=rq[:], in0=probs[:, 0:1024], scalar=1.0, in1=probs[:, 0:1024],
                        op0=ALU.mult, op1=ALU.mult,
                        accum_out=ssq_c[:, cj:cj + 1]), sem="V")
                    S.wait("scalar", tsq)
                    S.run("scalar", lambda e, cj=cj: e.activation(
                        out=std_c[:, cj:cj + 1], in_=ssq_c[:, cj:cj + 1],
                        func=AF.Ln, scale=1.0 / D, bias=eps_c[:]), sem="A")
                    S.drain("scalar")
                    trst = S.run("scalar", lambda e, cj=cj: e.activation(
                        out=rstd_c[:, cj:cj + 1], in_=std_c[:, cj:cj + 1],
                        func=AF.Exp, scale=-0.5), sem="A")
                    S.wait("vector", trst)
                    S.wait("vector", tok_rscol)
                    S.drain("vector")
                    S.run("vector", lambda e, cj=cj: e.tensor_scalar(
                        out=rsx_c[:, cj:cj + 1], in0=rstd_c[:, cj:cj + 1],
                        scalar1=rs_col[:], scalar2=None, op0=ALU.mult))
                    S.drain("vector")
                    tb2 = S.run("vector", lambda e, cj=cj: e.tensor_scalar(
                        out=b2_c[:, cj:cj + 1], in0=negmd_c[:, cj:cj + 1],
                        scalar1=rsx_c[:, cj:cj + 1], scalar2=128.5, op0=ALU.mult,
                        op1=ALU.add), sem="V")
                    S.wait("scalar", tb2)
                    tz = S.run("scalar", lambda e, j1=j1, cj=cj: e.activation(
                        out=probs[:, 0:1024], in_=x2[j1 % 2][:], func=AF.Identity,
                        scale=rsx_c[:, cj:cj + 1], bias=b2_c[:, cj:cj + 1]), sem="A")
                    tok_tz[j1] = tz


                if 0 <= j2 < NT:
                    S.wait("vector", tok_ff_c[(j2, NCH - 1)])
                    S.wait("vector", tok_scol)
                    S.wait("vector", tok_store.get(j2 - 1))
                    tok_ff_ev[j2] = S.run("vector", lambda e, j2=j2:
                                          e.scalar_tensor_tensor(
                        out=ostg[:], in0=ps_sc[:, 1024:2048], scalar=s_col[:],
                        in1=x2[j2 % 2][:], op0=ALU.mult, op1=ALU.add), sem="V")
                    S.wait("gpsimd", tok_ff_ev[j2])
                    tok_store[j2] = S.dma("gpsimd", lambda e, j2=j2, s=s: e.dma_start(
                        out=out_d[s, j2 * 128:(j2 + 1) * 128, :], in_=ostg[:]), "DG")

                if 0 <= j1 < NT:
                    S.wait("vector", tok_tz[j1])
                    S.wait("vector", tok_store.get(j1 - 1))
                    S.run("vector", lambda e: e.tensor_scalar(
                        out=probs[:, 0:1024], in0=probs[:, 0:1024], scalar1=0.9, scalar2=255.9,
                        op0=ALU.max, op1=ALU.min), sem="V")
                    S.run("vector", lambda e: e.tensor_copy(out=rq[:], in_=probs[:, 0:1024]),
                          sem="V")
                    for b in (128.0, 64.0, 32.0, 16.0, 8.0, 4.0, 2.0, 1.0):
                        S.run("vector", lambda e, b=b: e.tensor_scalar(
                            out=ostg[:], in0=rq[:], scalar1=b, scalar2=b,
                            op0=ALU.is_ge, op1=ALU.mult))
                        S.run("vector", lambda e: e.tensor_tensor(
                            out=rq[:], in0=rq[:], in1=ostg[:], op=ALU.subtract))
                    S.run("vector", lambda e: e.tensor_tensor(
                        out=ostg[:], in0=probs[:, 0:1024], in1=rq[:], op=ALU.subtract), sem="V")
                    S.wait("vector", tok_xt_ev.get(j1 - 2))
                    tok_xint[j1] = S.run("vector", lambda e, j1=j1: e.tensor_scalar(
                        out=xq32[j1 % 2][:], in0=ostg[:], scalar1=-128.0,
                        scalar2=None, op0=ALU.add), sem="V")


            prev_phase_toks = [S.now(x) for x in (("T", "A", "V") + DMAS)]

    S.wait("gpsimd", S.now("DG"))

    with es:
        with nc.Block() as block:
            S.realize(block, sems)
    return nc


_NC_CACHE = []


def kernel(L0_attn_x, L0_attn_QK, L0_attn_VO, L0_ffn_ternary_weight,
           L0_ffn_bias, L0_ffn_act_scale):
    x = np.ascontiguousarray(np.asarray(L0_attn_x, dtype=np.float32))
    qk = np.ascontiguousarray(np.asarray(L0_attn_QK, dtype=np.float32))
    vo = np.ascontiguousarray(np.asarray(L0_attn_VO, dtype=np.float32))
    wt = np.ascontiguousarray(np.asarray(L0_ffn_ternary_weight, dtype=np.int32))
    fb = np.ascontiguousarray(np.asarray(L0_ffn_bias, dtype=np.float32).reshape(1, D))
    asc = np.ascontiguousarray(np.asarray(L0_ffn_act_scale,
                                          dtype=np.float32).reshape(1, 1))
    idf = np.eye(128, dtype=np.float32)
    idb = np.eye(128, dtype=BF)
    onesb = np.ones((1, 128), dtype=BF)

    shards = x.reshape(8, NSEQ, T, D)
    in_maps = [{"x": shards[c], "qk": qk, "vo": vo, "wt": wt, "fbias": fb,
                "ascale": asc, "idf": idf, "idb": idb, "onesb": onesb}
               for c in range(8)]
    if not _NC_CACHE:
        _NC_CACHE.append(build_nc())
    nc = _NC_CACHE[0]
    res = run_bass_kernel_spmd(nc, in_maps, core_ids=list(range(8)))
    out = np.stack([np.asarray(r["out"]) for r in res.results], axis=0)
    return out.reshape(16, T, D)


if __name__ == "__main__":
    nc = build_nc()
    print("built", sum(1 for _ in nc.all_instructions()), "instructions")
